# version 15
# baseline (speedup 1.0000x reference)
import numpy as np
import ml_dtypes
import concourse.bass as bass
import concourse.mybir as mybir
from concourse.bass_utils import run_bass_kernel_spmd
from contextlib import ExitStack

F32 = mybir.dt.float32
BF16 = mybir.dt.bfloat16
AF = mybir.ActivationFunctionType
ALU = mybir.AluOpType
AX = mybir.AxisListType
BF = ml_dtypes.bfloat16

NCORES = 8
NB = 2
S = 2048
D = 1024
H = 16
HD = 64
NCH = 8
NT = 16
NQT = 4
EPS = 1e-5
SERIALIZE = False

ENGINES = ("tensor", "vector", "scalar", "sync", "gpsimd")
DMA_ENG = ("sync", "gpsimd")
STREAM_LANES = {"ld": 4, "st": 4, "w": 2}
LANES = tuple(f"{s}{i}" for s, n in STREAM_LANES.items() for i in range(n))


class Sched:

    def __init__(self, nc):
        self.nc = nc
        self.ops = {e: [] for e in ENGINES}
        self.cnt = {e: 0 for e in ENGINES}
        for s in LANES:
            self.cnt[s] = 0
        self.rr = {s: 0 for s in STREAM_LANES}
        self.extra_gate = []
        self.gate_ttl = 0
        self.sems = {}

    SELF_SERIAL = ("vector", "scalar")
    TSER_PHASES = set()
    phase = ""

    def add(self, eng, fn, waits=(), sig=False, stream=None):
        assert (stream is not None) == (eng in DMA_ENG)
        if self.extra_gate:
            waits = list(waits) + self.extra_gate
            self.gate_ttl -= 1
            if self.gate_ttl <= 0:
                self.extra_gate = []
        ws = [w for w in waits if w is not None and w[0] != eng]
        lane = None
        if eng in DMA_ENG:
            lane = f"{stream}{self.rr[stream] % STREAM_LANES[stream]}"
            self.rr[stream] += 1
            if self.cnt[lane] > 0:
                ws.append((lane, self.cnt[lane]))
        if eng in self.SELF_SERIAL:
            if self.cnt[eng] > 0:
                ws.append((eng, self.cnt[eng]))
            sig = True
        if SERIALIZE:
            sig = True
            ws2 = [(e, self.cnt[e]) for e in ENGINES + LANES
                   if self.cnt[e] > 0 and e != eng and e != lane]
            if eng in self.SELF_SERIAL and self.cnt[eng] > 0:
                ws2.append((eng, self.cnt[eng]))
            if lane is not None and self.cnt[lane] > 0:
                ws2.append((lane, self.cnt[lane]))
            ws = ws2
        self.ops[eng].append((fn, ws, sig, lane))
        if sig:
            if eng in DMA_ENG:
                self.cnt[lane] += 16
                return (lane, self.cnt[lane])
            self.cnt[eng] += 1
            return (eng, self.cnt[eng])
        return None

    def token(self, eng):
        return (eng, self.cnt[eng]) if self.cnt[eng] > 0 else None

    def barrier_tokens(self):
        return [self.token(e) for e in ENGINES + LANES
                if e not in DMA_ENG and self.cnt[e] > 0]

    def flush(self):
        with self.nc.Block(no_gpsimd_drain=True) as block:
            for eng in ENGINES:
                ops = self.ops[eng]
                if not ops:
                    continue

                def make_emit(eng, ops):
                    def emit(h):
                        waited = {}
                        for fn, ws, sig in ops:
                            for (we, wv) in ws:
                                if waited.get(we, -1) < wv:
                                    h.wait_ge(self.sems[we], wv)
                                    waited[we] = wv
                            fn(h)
                            if sig:
                                pass
                    return emit

                def make_emit2(eng, ops):
                    def emit(h):
                        waited = {}
                        for fn, ws, sig, stream in ops:
                            for (we, wv) in ws:
                                if waited.get(we, -1) < wv:
                                    h.wait_ge(self.sems[we], wv)
                                    waited[we] = wv
                            ins = fn(h)
                            if sig:
                                if eng in DMA_ENG:
                                    ins.then_inc(self.sems[stream], 16)
                                else:
                                    ins.then_inc(self.sems[eng], 1)
                    return emit

                getattr(block, eng)(make_emit2(eng, ops))
                self.ops[eng] = []


def head_perm():
    p = np.empty(D, np.int64)
    for h in range(H):
        for j in range(HD):
            p[HD * h + j] = 16 * j + h
    return p


def build_consts():
    idb = np.eye(128, dtype=BF)
    idf = np.eye(128, dtype=np.float32)
    keep = np.triu(np.ones((128, 128), np.float32)).astype(BF)
    ones_row = np.ones((1, 128), BF)
    ones128 = np.ones((128, 128), BF)
    ind = np.zeros((128, NCH, 16), np.float32)
    for c in range(NCH):
        ind[0:64, c, 2 * c] = 1.0
        ind[64:128, c, 2 * c + 1] = 1.0
    ind = ind.reshape(128, 128).astype(BF)
    return idb, idf, keep, ones_row, ind, ones128


def build_program(debug=False):
    nc = bass.Bass("TRN2", target_bir_lowering=False)
    dbg = {}
    def dbg_out(name, shape, dtype):
        if debug and name not in dbg:
            dbg[name] = nc.dram_tensor("dbg_" + name, shape, dtype, kind="ExternalOutput").ap()
        return dbg.get(name)

    x_d = nc.dram_tensor("x", [NB * S, D], F32, kind="ExternalInput").ap()
    qw_d = nc.dram_tensor("qw", [D, D], F32, kind="ExternalInput").ap()
    kw_d = nc.dram_tensor("kw", [D, D], F32, kind="ExternalInput").ap()
    vw_d = nc.dram_tensor("vw", [D, D], F32, kind="ExternalInput").ap()
    ow_d = nc.dram_tensor("ow", [D, D], F32, kind="ExternalInput").ap()
    w1_d = nc.dram_tensor("w1", [D, D], F32, kind="ExternalInput").ap()
    w2_d = nc.dram_tensor("w2", [D, 512], F32, kind="ExternalInput").ap()
    w3_d = nc.dram_tensor("w3", [512, 256], F32, kind="ExternalInput").ap()
    w4_d = nc.dram_tensor("w4", [256, 1], F32, kind="ExternalInput").ap()
    g1_d = nc.dram_tensor("g1", [D, 512], F32, kind="ExternalInput").ap()
    g2_d = nc.dram_tensor("g2", [512, 256], F32, kind="ExternalInput").ap()
    g3_d = nc.dram_tensor("g3", [256, 1], F32, kind="ExternalInput").ap()
    b1_d = nc.dram_tensor("b1", [D], F32, kind="ExternalInput").ap()
    b2_d = nc.dram_tensor("b2", [512], F32, kind="ExternalInput").ap()
    b3_d = nc.dram_tensor("b3", [256], F32, kind="ExternalInput").ap()
    sc_d = nc.dram_tensor("sc", [1, 4], F32, kind="ExternalInput").ap()
    sc2_d = nc.dram_tensor("sc2", [1, 2], F32, kind="ExternalInput").ap()
    gb1_d = nc.dram_tensor("gb1", [512], F32, kind="ExternalInput").ap()
    gb2_d = nc.dram_tensor("gb2", [256], F32, kind="ExternalInput").ap()
    cidb_d = nc.dram_tensor("cidb", [128, 128], BF16, kind="ExternalInput").ap()
    cidf_d = nc.dram_tensor("cidf", [128, 128], F32, kind="ExternalInput").ap()
    ckeep_d = nc.dram_tensor("ckeep", [128, 128], BF16, kind="ExternalInput").ap()
    cones_d = nc.dram_tensor("cones", [1, 128], BF16, kind="ExternalInput").ap()
    cind_d = nc.dram_tensor("cind", [128, 128], BF16, kind="ExternalInput").ap()
    cone2_d = nc.dram_tensor("cone2", [128, 128], BF16, kind="ExternalInput").ap()
    y_d = nc.dram_tensor("y", [NB * S, D], F32, kind="ExternalOutput").ap()
    x1_d = nc.dram_tensor("x1buf", [NB * S, D], F32, kind="Internal").ap()

    es = ExitStack()
    es.enter_context(nc.allow_low_precision(
        "deliberate bf16 matmul-operand casts; fp32 accumulation kept in psum"))
    sch = Sched(nc)
    for e in ENGINES + LANES:
        if e in DMA_ENG:
            continue
        sch.sems[e] = es.enter_context(nc.semaphore("sem_" + e))
    _uid = [0]

    def _un(name):
        _uid[0] += 1
        return f"{name}_{_uid[0]}"

    c_idb = es.enter_context(nc.sbuf_tensor(_un("c_idb"), [128, 128], BF16))
    c_idf = es.enter_context(nc.sbuf_tensor(_un("c_idf"), [128, 128], F32))
    c_keep = es.enter_context(nc.sbuf_tensor(_un("c_keep"), [128, 128], BF16))
    c_ones = es.enter_context(nc.sbuf_tensor(_un("c_ones"), [1, 128], BF16))
    c_ind = es.enter_context(nc.sbuf_tensor(_un("c_ind"), [128, 128], BF16))
    c_one2 = es.enter_context(nc.sbuf_tensor(_un("c_one2"), [128, 128], BF16))
    b1_sb = es.enter_context(nc.sbuf_tensor(_un("b1_sb"), [128, 8], F32))
    b2_sb = es.enter_context(nc.sbuf_tensor(_un("b2_sb"), [128, 4], F32))
    b3_sb = es.enter_context(nc.sbuf_tensor(_un("b3_sb"), [128, 2], F32))
    gb1_sb = es.enter_context(nc.sbuf_tensor(_un("gb1_sb"), [128, 4], F32))
    gb2_sb = es.enter_context(nc.sbuf_tensor(_un("gb2_sb"), [128, 2], F32))
    sc_sb = es.enter_context(nc.sbuf_tensor(_un("sc_sb"), [1, 4], F32))
    sc_sb2 = es.enter_context(nc.sbuf_tensor(_un("sc_sb2"), [1, 2], F32))
    stat6 = es.enter_context(nc.sbuf_tensor(_un("stat6"), [128, 12], F32))
    aggs = es.enter_context(nc.sbuf_tensor(_un("aggs"), [128, 144], F32))
    lncol = es.enter_context(nc.sbuf_tensor(_un("lncol"), [128, 160], F32))
    negB = es.enter_context(nc.sbuf_tensor(_un("negB"), [128, 16], F32))
    srow = es.enter_context(nc.sbuf_tensor(_un("srow"), [16, 8], F32))
    Bb = es.enter_context(nc.sbuf_tensor(_un("Bb"), [16, 1], BF16))
    brow = es.enter_context(nc.sbuf_tensor(_un("brow"), [1, 16], BF16))
    tfbuf = es.enter_context(nc.sbuf_tensor(_un("tfbuf"), [1, S], F32))
    gcp = es.enter_context(nc.sbuf_tensor(_un("gcp"), [128, 32], F32))
    gcc = es.enter_context(nc.sbuf_tensor(_un("gcc"), [128, 8], BF16))
    g1gb = es.enter_context(nc.sbuf_tensor(_un("g1gb"), [128, 4], BF16))
    g2gb = es.enter_context(nc.sbuf_tensor(_un("g2gb"), [128, 2], BF16))
    scal = es.enter_context(nc.sbuf_tensor(_un("scal"), [1, 8], F32))
    crow = es.enter_context(nc.sbuf_tensor(_un("crow"), [1, 512], F32))
    crowb = es.enter_context(nc.sbuf_tensor(_un("crowb"), [1, 512], BF16))
    ccol = es.enter_context(nc.sbuf_tensor(_un("ccol"), [128, 4], F32))
    nccol = es.enter_context(nc.sbuf_tensor(_un("nccol"), [128, 4], F32))
    tmpq = es.enter_context(nc.sbuf_tensor(_un("tmpq"), [128, 512], F32))

    def dma(dst, src):
        return lambda h: h.dma_start(out=dst, in_=src)

    def dma_nc(dst, src):
        return lambda h: h.dma_start(out=dst, in_=src, allow_slow_non_contiguous=True)

    tok_c = None
    if debug:
        sch.add("vector", lambda h: h.memset(aggs[:], 0.0))
        sch.add("vector", lambda h: h.memset(lncol[:], 0.0))
    for dst, src in ((c_idb[:], cidb_d), (c_idf[:], cidf_d), (c_keep[:], ckeep_d),
                     (c_ones[:], cones_d), (c_ind[:], cind_d), (c_one2[:], cone2_d),
                     (sc_sb[:], sc_d), (sc_sb2[:], sc2_d)):
        tok_c = sch.add("sync", dma(dst, src), sig=True, stream="ld")
    for dst, src in ((b1_sb[:], b1_d.rearrange("(d p) -> p d", p=128)),
                     (b2_sb[:], b2_d.rearrange("(d p) -> p d", p=128)),
                     (b3_sb[:], b3_d.rearrange("(d p) -> p d", p=128)),
                     (gb1_sb[:], gb1_d.rearrange("(d p) -> p d", p=128)),
                     (gb2_sb[:], gb2_d.rearrange("(d p) -> p d", p=128))):
        tok_c = sch.add("sync", dma_nc(dst, src), sig=True, stream="ld")

    def ln_stats(src_ap, i, wait):
        sch.add("vector", lambda h, a=src_ap: h.bn_stats(stat6[:, 0:6], a[:, 0:512]),
                waits=wait)
        sch.add("vector", lambda h, a=src_ap: h.bn_stats(stat6[:, 6:12], a[:, 512:1024]))
        sch.add("vector", lambda h, i=i: h.bn_aggr(aggs[:, 4 * i:4 * i + 2], stat6[:, 0:6]))
        return sch.add("vector", lambda h, i=i: h.bn_aggr(aggs[:, 4 * i + 2:4 * i + 4],
                                                          stat6[:, 6:12]), sig=True)

    def ln_combine(lo, n):
        av = aggs[:].rearrange("p (i k) -> p i k", k=4)
        mA, vA = av[:, lo:lo + n, 0:1], av[:, lo:lo + n, 1:2]
        mB, vB = av[:, lo:lo + n, 2:3], av[:, lo:lo + n, 3:4]
        def flat(a):
            return a.rearrange("p i k -> p (i k)")
        m = lncol[:, 0 + lo:0 + lo + n]
        var = lncol[:, 40 + lo:40 + lo + n]
        nmr = lncol[:, 80 + lo:80 + lo + n]
        r = lncol[:, 120 + lo:120 + lo + n]
        sch.add("vector", lambda h: h.tensor_add(m, flat(mA), flat(mB)))
        sch.add("vector", lambda h: h.tensor_add(var, flat(vA), flat(vB)))
        sch.add("vector", lambda h: h.tensor_sub(nmr, flat(mA), flat(mB)))
        sch.add("vector", lambda h: h.tensor_mul(nmr, nmr, nmr))
        sch.add("vector", lambda h: h.tensor_scalar(m, m, 0.5, None, op0=ALU.mult))
        sch.add("vector", lambda h: h.tensor_scalar(var, var, 0.5, None, op0=ALU.mult))
        sch.add("vector", lambda h: h.scalar_tensor_tensor(var, nmr, 0.25, var,
                                                           op0=ALU.mult, op1=ALU.add))
        t0 = sch.add("vector", lambda h: h.tensor_scalar(var, var, EPS, None,
                                                         op0=ALU.add), sig=True)
        sch.add("scalar", lambda h: h.activation(r, var, AF.Ln), waits=[t0])
        t1 = sch.add("scalar", lambda h: h.activation(r, r, AF.Exp, scale=-0.5),
                     sig=True)
        return sch.add("vector", lambda h: h.scalar_tensor_tensor(nmr, m, -1.0, r,
                                                                  op0=ALU.mult, op1=ALU.mult),
                       waits=[t1], sig=True)

    def lnc(i, which):
        base = {"nmr": 80, "r": 120}[which]
        return lncol[:, base + i:base + i + 1]

    xt = [es.enter_context(nc.sbuf_tensor(_un(f"xt{k}"), [128, D], F32)) for k in range(2)]

    def emit_passA(b, gate):
        st_tok = [None] * NT
        for i in range(NT):
            w = [st_tok[i - 2]] if i >= 2 else list(gate)
            ld = sch.add("sync", dma(xt[i % 2][:],
                                     x_d[b * S + 128 * i:b * S + 128 * (i + 1), :]),
                         waits=w, sig=True, stream="ld")
            st_tok[i] = ln_stats(xt[i % 2][:], 20 + i, [ld])
        comb = ln_combine(20, NT)
        return st_tok, comb

    wbuf_use = [None, None]
    xres_use = [None, None]
    prev_batch_done = []
    prev_st_toks = []
    prev_se_tok = None
    prev_ve_tok = None

    for b in range(NB):
        xrow0 = b * S
        st_attn = ExitStack()
        Qf = st_attn.enter_context(nc.sbuf_tensor(_un("Qf"), [128, NCH * S], BF16))
        Kf = st_attn.enter_context(nc.sbuf_tensor(_un("Kf"), [128, NCH * S], BF16))
        Vp = st_attn.enter_context(nc.sbuf_tensor(_un("Vp"), [128, NT * H * 65], BF16))
        wo_sb = st_attn.enter_context(nc.sbuf_tensor(_un("wo_sb"), [128, NCH * D], BF16))
        Qv = Qf[:].rearrange("p (c t) -> p c t", t=S)
        Kv = Kf[:].rearrange("p (c t) -> p c t", t=S)
        Vv = Vp[:].rearrange("p (kb z) -> p kb z", z=H * 65)
        vz = Vp[:].rearrange("p (kb g z) -> p kb g z", g=H, z=65)
        Wov = wo_sb[:].rearrange("p (c n) -> p c n", n=D)

        p1 = ExitStack()
        hfeat = p1.enter_context(nc.sbuf_tensor(_un("hfeat"), [128, NCH * S], BF16))
        hv = hfeat[:].rearrange("p (c t) -> p c t", t=S)
        htok = [p1.enter_context(nc.sbuf_tensor(_un(f"htok{k}"), [128, D], BF16)) for k in range(2)]
        wbuf = [p1.enter_context(nc.sbuf_tensor(_un(f"wbuf{k}"), [128, 4096], BF16)) for k in range(2)]
        qsq = [p1.enter_context(nc.sbuf_tensor(_un(f"qsq{k}"), [128, S], BF16)) for k in range(2)]
        pp1 = ExitStack()
        pQ = [pp1.enter_context(nc.psum_tensor(_un(f"pQ{k}"), [128, 1024], F32)) for k in range(2)]
        pT = [pp1.enter_context(nc.psum_tensor(_un(f"pT{k}"), [128, 512], BF16)) for k in range(2)]

        sch.phase = 'p1tr'
        pT_use = [prev_ve_tok, prev_ve_tok]
        if b == 0:
            st_tok, comb1 = emit_passA(0, [tok_c])
        else:
            st_tok, comb1 = passA_next
        if debug and b == 0:
            o = dbg_out("aggs", [128, 144], F32)
            sch.gate_ttl = 400; sch.extra_gate.append(sch.add("sync", dma(o, aggs[:]), waits=[comb1], sig=True, stream="st"))
            o = dbg_out("lncol", [128, 160], F32)
            sch.gate_ttl = 400; sch.extra_gate.append(sch.add("sync", dma(o, lncol[:]), waits=[comb1], sig=True, stream="st"))
            o = dbg_out("xt0", [128, D], F32)
            sch.gate_ttl = 400; sch.extra_gate.append(sch.add("sync", dma(o, xt[1][:]), waits=[comb1], sig=True, stream="st"))
        aff_tok = [None] * NT
        tr_all = [None] * NT
        h_done = None
        for i in range(NT):
            w = [aff_tok[i - 2]] if i >= 2 else [st_tok[NT - 1]]
            ld = sch.add("sync", dma(xt[i % 2][:], x_d[xrow0 + 128 * i:xrow0 + 128 * (i + 1), :]),
                         waits=w, sig=True, stream="ld")
            aff_tok[i] = sch.add("scalar",
                                 lambda h, i=i: h.activation(htok[i % 2][:], xt[i % 2][:],
                                                             AF.Identity,
                                                             bias=lnc(20 + i, "nmr"),
                                                             scale=lnc(20 + i, "r")),
                                 waits=[ld, comb1, tr_all[i - 2] if i >= 2 else None],
                                 sig=True)
            for half in range(2):
                for c4 in range(4):
                    c = 4 * half + c4
                    tr = sch.add("tensor",
                            lambda h, i=i, c=c, half=half, c4=c4: h.transpose(
                                pT[half][:, 128 * c4:128 * (c4 + 1)],
                                htok[i % 2][:, 128 * c:128 * (c + 1)], c_idb[:]),
                            waits=[aff_tok[i], pT_use[half]] if c4 == 0 else [],
                            sig=(c4 == 3))
                pT_use[half] = sch.add("vector",
                                       lambda h, i=i, half=half: h.tensor_copy(
                                           hv[:, 4 * half:4 * half + 4, 128 * i:128 * (i + 1)],
                                           pT[half][:].rearrange("p (g c) -> p g c", c=128)),
                                       waits=[tr], sig=True)
            tr_all[i] = sch.token("tensor")
        h_done = sch.token("vector")

        def wsrc(wd, half):
            return wd.rearrange("(d p) n -> p d n", p=128)[:, :, 512 * half:512 * (half + 1)]

        sch.phase = 'proj'
        ones_set = sch.add("vector", lambda h: h.memset(vz[:, :, :, 64:65], 1.0),
                           waits=[tok_c], sig=True)
        units = [(wi, wd, half) for wi, wd in enumerate((qw_d, kw_d, vw_d))
                 for half in range(2)]
        load_tok = [None] * 6
        slot_mm_done = wbuf_use

        def issue_load(u):
            wi, wd, half = units[u]
            slot = u % 2
            load_tok[u] = sch.add(
                "gpsimd",
                dma(wbuf[slot][:].rearrange("p (d n) -> p d n", n=512), wsrc(wd, half)),
                waits=[slot_mm_done[slot]] + prev_batch_done, sig=True, stream="w")

        issue_load(0)
        issue_load(1)
        wo_ld = sch.add("gpsimd", dma(Wov, ow_d.rearrange("(c p) n -> p c n", p=128)),
                        waits=prev_batch_done, sig=True, stream="w")
        pq_use = [[prev_se_tok] + prev_st_toks, [prev_se_tok] + prev_st_toks]
        for u in range(6):
            wi, wd, half = units[u]
            slot = u % 2
            wb = wbuf[slot][:].rearrange("p (d n) -> p d n", n=512)
            wtk = load_tok[u]
            if wi < 2:
                dstv = Qv if wi == 0 else Kv
                for jp in range(2):
                    for tt in range(4):
                        pslot = (jp * 4 + tt) % 2
                        ps = pQ[pslot]
                        for jj in range(2):
                            jt = 2 * jp + jj
                            for d in range(NCH):
                                mm = sch.add("tensor",
                                        lambda h, wb=wb, d=d, jt=jt, jj=jj, tt=tt, ps=ps: h.matmul(
                                            ps[:, 512 * jj:512 * (jj + 1)],
                                            wb[:, d, 128 * jt:128 * (jt + 1)],
                                            hv[:, d, 512 * tt:512 * (tt + 1)],
                                            start=(d == 0), stop=(d == NCH - 1)),
                                        waits=([wtk, h_done] + (pq_use[pslot] if isinstance(pq_use[pslot], list) else [pq_use[pslot]])) if (d == 0 and jj == 0) else [],
                                        sig=(d == NCH - 1 and jj == 1))
                        dout = 4 * half + 2 * jp
                        pq_use[pslot] = sch.add(
                            "vector",
                            lambda h, dstv=dstv, dout=dout, tt=tt, ps=ps: h.tensor_copy(
                                dstv[:, dout:dout + 2, 512 * tt:512 * (tt + 1)],
                                ps[:].rearrange("p (g n) -> p g n", n=512)),
                            waits=[mm], sig=True)
            else:
                for kb in range(NT):
                    pslot = kb % 2
                    ps = pQ[pslot]
                    for d in range(NCH):
                        mm = sch.add("tensor",
                                lambda h, wb=wb, d=d, kb=kb, ps=ps: h.matmul(
                                    ps[:, 0:512], hv[:, d, 128 * kb:128 * (kb + 1)], wb[:, d, :],
                                    start=(d == 0), stop=(d == NCH - 1)),
                                waits=([wtk, h_done, ones_set] + (pq_use[pslot] if isinstance(pq_use[pslot], list) else [pq_use[pslot]])) if d == 0 else [],
                                sig=(d == NCH - 1))
                    pq_use[pslot] = sch.add(
                        "vector",
                        lambda h, kb=kb, half=half, ps=ps: h.tensor_copy(
                            vz[:, kb, 8 * half:8 * (half + 1), 0:64],
                            ps[:, 0:512].rearrange("p (g c) -> p g c", c=64)),
                        waits=[mm], sig=True)
            slot_mm_done[slot] = sch.token("tensor")
            if u + 2 < 6:
                issue_load(u + 2)
        v_done = sch.token("vector")
        wbuf_use = slot_mm_done

        sch.phase = 'p1b'
        pp1.close()
        pp1b = ExitStack()
        pN = [pp1b.enter_context(nc.psum_tensor(_un(f"pN{k}"), [16, 512], F32)) for k in range(4)]
        pBB = pp1b.enter_context(nc.psum_tensor(_un("pBB"), [128, 16], F32))
        pBr = pp1b.enter_context(nc.psum_tensor(_un("pBr"), [1, 16], BF16))
        indv = c_ind[:].rearrange("p (c z) -> p c z", z=16)
        prev_side = None
        prev_mm = [None, None]
        for src, col in ((Qv, 0), (Kv, 1)):
            sq_tok = [None, None]
            mm_tok = [None] * NCH
            for c in range(NCH):
                sq_tok[c % 2] = sch.add("scalar",
                                        lambda h, src=src, c=c: h.activation(
                                            qsq[c % 2][:], src[:, c, :], AF.Square),
                                        waits=[v_done,
                                               mm_tok[c - 2] if c >= 2 else prev_mm[c % 2]],
                                        sig=True)
                for tt in range(4):
                    mm_tok[c] = sch.add("tensor",
                            lambda h, c=c, tt=tt: h.matmul(
                                pN[tt][:], indv[:, c, :],
                                qsq[c % 2][:, 512 * tt:512 * (tt + 1)],
                                start=(c == 0), stop=(c == NCH - 1)),
                            waits=[sq_tok[c % 2], v_done, prev_side] if tt == 0 else [],
                            sig=(tt == 3))
            for tt in range(4):
                sch.add("vector",
                        lambda h, tt=tt: h.reduce_max(srow[0:16, 4 + tt:5 + tt],
                                                      pN[tt][:], axis=AX.X),
                        waits=[mm_tok[NCH - 1]])
            sch.add("vector", lambda h: h.tensor_max(srow[0:16, 4:5], srow[0:16, 4:5],
                                                     srow[0:16, 5:6]))
            sch.add("vector", lambda h: h.tensor_max(srow[0:16, 6:7], srow[0:16, 6:7],
                                                     srow[0:16, 7:8]))
            prev_side = sch.add("vector",
                                lambda h, col=col: h.tensor_max(srow[0:16, col:col + 1],
                                                                srow[0:16, 4:5],
                                                                srow[0:16, 6:7]), sig=True)
            prev_mm = [mm_tok[NCH - 2], mm_tok[NCH - 1]]
        t0 = sch.add("vector", lambda h: h.tensor_mul(srow[0:16, 2:3], srow[0:16, 0:1],
                                                      srow[0:16, 1:2]), sig=True)
        sch.add("scalar", lambda h: h.activation(srow[0:16, 3:4], srow[0:16, 2:3],
                                                 AF.Ln, scale=1.0 / 1024.0),
                waits=[t0])
        t1 = sch.add("scalar", lambda h: h.activation(srow[0:16, 3:4], srow[0:16, 3:4],
                                                      AF.Exp, scale=0.5),
                     waits=[], sig=True)
        t2 = sch.add("vector", lambda h: h.tensor_copy(Bb[:], srow[0:16, 3:4]),
                     waits=[t1], sig=True)
        t3 = sch.add("tensor", lambda h: h.transpose(pBr[:], Bb[:], c_idb[0:16, 0:16]),
                     waits=[t2], sig=True)
        t4 = sch.add("vector", lambda h: h.tensor_copy(brow[:], pBr[:]),
                     waits=[t3], sig=True)
        t5 = sch.add("tensor", lambda h: h.matmul(pBB[:], c_ones[:], brow[:],
                                                  start=True, stop=True),
                     waits=[t4], sig=True)
        negB_tok = sch.add("vector",
                           lambda h: h.tensor_scalar(negB[:], pBB[:], -1.0, -1.0,
                                                     op0=ALU.mult, op1=ALU.add),
                           waits=[t5], sig=True)
        if debug and b == 0:
            for nm, t, shp, dt in (("h", hfeat, [128, NCH * S], BF16),
                                   ("q", Qf, [128, NCH * S], BF16),
                                   ("k", Kf, [128, NCH * S], BF16),
                                   ("v", Vp, [128, NT * H * 65], BF16),
                                   ("negB", negB, [128, 16], F32)):
                o = dbg_out(nm, shp, dt)
                sch.gate_ttl = 400; sch.extra_gate.append(sch.add("sync", dma(o, t[:]), waits=[negB_tok, h_done],
                                              sig=True, stream="st"))
        sch.flush()
        pp1b.close()
        p1.close()

        sch.phase = 'p2'
        p2 = ExitStack()
        ringA = p2.enter_context(nc.sbuf_tensor(_un("ringA"), [128, 4 * 512], BF16))
        ringB = p2.enter_context(nc.sbuf_tensor(_un("ringB"), [128, 4 * 512], BF16))
        attnU = p2.enter_context(nc.sbuf_tensor(_un("attnU"), [128, 8 * 512], BF16))
        stage = p2.enter_context(nc.sbuf_tensor(_un("stage"), [64, 512], BF16))
        lbuf = p2.enter_context(nc.sbuf_tensor(_un("lbuf"), [65, 8 * 1024], BF16))
        rlb = p2.enter_context(nc.sbuf_tensor(_un("rlb"), [65, 8 * 1024], BF16))
        lscr = p2.enter_context(nc.sbuf_tensor(_un("lscr"), [65, 1024], F32))
        otmp = p2.enter_context(nc.sbuf_tensor(_un("otmp"), [128, 1024], F32))
        xres = [p2.enter_context(nc.sbuf_tensor(_un(f"xres{k}"), [128, D], F32)) for k in range(2)]
        attv = attnU[:].rearrange("p (g n) -> p g n", n=512)
        ringAv = ringA[:].rearrange("p (g n) -> p g n", n=512)
        ringBv = ringB[:].rearrange("p (g n) -> p g n", n=512)

        expA_done = [None, None]
        expB_done = [None, None]
        ringA_use = [None] * 4
        ringB_use = [None] * 4
        pv_free = None
        dmaB_prev = None
        o_use = [None, None]
        xres_stat = [None, None]
        rl_done = [None] * 8
        lb_use = [None] * 8
        lnl_prev = [None] * 8
        x1_store_tok = [None] * NT
        tt_carry = [None]
        ev_o = [None, None]
        rl_scr = None
        ppS = ExitStack()
        pSA = [ppS.enter_context(nc.psum_tensor(_un(f"pSA{k}"), [128, 512], F32)) for k in range(2)]
        pSB = [ppS.enter_context(nc.psum_tensor(_un(f"pSB{k}"), [128, 512], F32)) for k in range(2)]
        pPVA = ppS.enter_context(nc.psum_tensor(_un("pPVA"), [65, 512], F32))
        pPVB = ppS.enter_context(nc.psum_tensor(_un("pPVB"), [65, 512], F32))
        pO = [ppS.enter_context(nc.psum_tensor(_un(f"pO{k}"), [128, 512], F32)) for k in range(2)]
        for j in range(NQT):
            nch_j = 4 * j + 4
            ev_toks = []
            def s_mm(p, c, j=j):
                nonlocal s_tok
                ss = c % 2
                sch.add("tensor",
                        lambda h, p=p, c=c, ss=ss, j=j: h.matmul(
                            pSA[ss][:],
                            Kv[0:64, p, 128 * c:128 * (c + 1)],
                            Qv[0:64, p, 512 * j:512 * (j + 1)],
                            start=True, stop=True),
                        waits=[expA_done[ss], negB_tok])
                s_tok[c] = sch.add("tensor",
                        lambda h, p=p, c=c, ss=ss, j=j: h.matmul(
                            pSB[ss][:],
                            Kv[64:128, p, 128 * c:128 * (c + 1)],
                            Qv[64:128, p, 512 * j:512 * (j + 1)],
                            start=True, stop=True, tile_position=(64, 0)),
                        waits=[expB_done[ss]], sig=True)
            for p in range(8):
                hA, hB = 2 * p, 2 * p + 1
                pv_last = None
                s_tok = {}
                s_mm(p, 0)
                for c in range(nch_j):
                    ss = c % 2
                    r = c % 4
                    if c + 1 < nch_j:
                        s_mm(p, c + 1)
                    eA = sch.add("scalar",
                                 lambda h, r=r, ss=ss, hA=hA: h.activation(
                                     ringAv[:, r, :], pSA[ss][:], AF.Exp,
                                     bias=negB[:, hA:hA + 1], scale=1.0 / 32.0),
                                 waits=[s_tok[c], ringA_use[r]], sig=True)
                    expA_done[ss] = eA
                    eB = sch.add("scalar",
                                 lambda h, r=r, ss=ss, hB=hB: h.activation(
                                     ringBv[:, r, :], pSB[ss][:], AF.Exp,
                                     bias=negB[:, hB:hB + 1], scale=1.0 / 32.0),
                                 waits=[ringB_use[r]], sig=True)
                    expB_done[ss] = eB
                    rd = c - 4 * j
                    if 0 <= rd <= 3:
                        for ring in (ringAv, ringBv):
                            if rd > 0:
                                sch.add("vector",
                                        lambda h, ring=ring, r=r, rd=rd: h.memset(
                                            ring[:, r, 0:128 * rd], 0.0),
                                        waits=[eB])
                            sch.add("vector",
                                    lambda h, ring=ring, r=r, rd=rd: h.tensor_mul(
                                        ring[:, r, 128 * rd:128 * (rd + 1)],
                                        ring[:, r, 128 * rd:128 * (rd + 1)],
                                        c_keep[:]),
                                    waits=[eB])
                        mk = sch.add("vector", lambda h: h.nop(), sig=True)
                        eA = mk
                        eB = mk
                    sch.add("tensor",
                            lambda h, c=c, r=r, hA=hA, nch_j=nch_j: h.matmul(
                                pPVA[:], Vv[:, c, 65 * hA:65 * hA + 65],
                                ringAv[:, r, :],
                                start=(c == 0), stop=(c == nch_j - 1)),
                            waits=[eA, pv_free] if c == 0 else [eA])
                    pv_last = sch.add("tensor",
                            lambda h, c=c, r=r, hB=hB, nch_j=nch_j: h.matmul(
                                pPVB[:], Vv[:, c, 65 * hB:65 * hB + 65],
                                ringBv[:, r, :],
                                start=(c == 0), stop=(c == nch_j - 1)),
                            waits=[eB] if c == 0 else [eB], sig=True)
                    ringA_use[r] = pv_last
                    ringB_use[r] = pv_last
                sch.add("vector",
                        lambda h, p=p: h.tensor_copy(
                            lbuf[64:65, 1024 * p:1024 * p + 512], pPVA[64:65, :]),
                        waits=[pv_last, lb_use[p], lnl_prev[p]])
                sch.add("vector",
                        lambda h, p=p: h.tensor_copy(
                            lbuf[64:65, 1024 * p + 512:1024 * p + 1024], pPVB[64:65, :]))
                evA = sch.add("vector",
                              lambda h, p=p: h.tensor_copy(attv[0:64, p, :], pPVA[0:64, :]),
                              sig=True)
                evB = sch.add("vector",
                              lambda h: h.tensor_copy(stage[:], pPVB[0:64, :]),
                              waits=[dmaB_prev], sig=True)
                pv_free = evB
                dmaB_prev = sch.add("sync",
                                    lambda h, p=p: h.dma_start(out=attv[64:128, p, :],
                                                               in_=stage[:]),
                                    waits=[evB], sig=True, stream="st")
                ev_toks.append((evA, dmaB_prev))
                rl_done[p] = sch.token("vector")
            sch.phase = 'oproj'
            pBC = pO[0]
            lnl_tok = [None] * 8
            for p in range(8):
                sch.add("scalar",
                        lambda h, p=p: h.activation(
                            lscr[64:65, :],
                            lbuf[64:65, 1024 * p:1024 * p + 1024], AF.Ln),
                        waits=[rl_done[p], lnl_tok[p - 1] if p else None])
                lnl_tok[p] = sch.add("scalar",
                        lambda h, p=p: h.activation(
                            rlb[64:65, 1024 * p:1024 * p + 1024],
                            lscr[64:65, :], AF.Exp, scale=-1.0),
                        sig=True)
            tt_prev = tt_carry[0]
            for p in range(8):
                bmA = sch.add("tensor",
                              lambda h, p=p: h.matmul(
                                  pBC[:], c_one2[64:65, :],
                                  rlb[64:65, 1024 * p:1024 * p + 512],
                                  start=True, stop=True),
                              waits=[lnl_tok[p], tt_prev, ev_o[0]], sig=True)
                ttA = sch.add("vector",
                              lambda h, p=p: h.tensor_mul(attv[0:64, p, :],
                                                          attv[0:64, p, :], pBC[0:64, :]),
                              waits=[bmA, ev_toks[p][0]], sig=True)
                bmB = sch.add("tensor",
                              lambda h, p=p: h.matmul(
                                  pBC[:], c_one2[64:65, :],
                                  rlb[64:65, 1024 * p + 512:1024 * p + 1024],
                                  start=True, stop=True),
                              waits=[ttA], sig=True)
                tt_prev = sch.add("vector",
                                  lambda h, p=p: h.tensor_mul(attv[64:128, p, :],
                                                              attv[64:128, p, :],
                                                              pBC[64:128, :]),
                                  waits=[bmB, ev_toks[p][1]], sig=True)
                lb_use[p] = bmB
                lnl_prev[p] = lnl_tok[p]
            tt_carry[0] = tt_prev
            attn_ready = [tt_prev, dmaB_prev]
            for qq in range(4):
                for dh in range(2):
                    k = 2 * qq + dh
                    sl_o = k % 2
                    for pr in range(8):
                        mo_tok = sch.add("tensor",
                                lambda h, qq=qq, dh=dh, pr=pr, sl_o=sl_o: h.matmul(
                                    pO[sl_o][:],
                                    attv[:, pr, 128 * qq:128 * (qq + 1)],
                                    Wov[:, pr, 512 * dh:512 * (dh + 1)],
                                    start=(pr == 0), stop=(pr == 7)),
                                waits=(attn_ready + [wo_ld, ev_o[sl_o]]) if pr == 0 else [],
                                sig=(pr == 7))
                    ev_o[sl_o] = sch.add("vector",
                            lambda h, dh=dh, sl_o=sl_o: h.tensor_copy(
                                otmp[:, 512 * dh:512 * (dh + 1)], pO[sl_o][:]),
                            waits=[mo_tok], sig=True)
                for dh in range(2):
                    sch.add("vector",
                            lambda h, dh=dh: h.bn_stats(stat6[:, 0:6],
                                                        otmp[:, 512 * dh:512 * (dh + 1)]))
                    sch.add("vector",
                            lambda h, qq=qq, dh=dh: h.bn_aggr(
                                aggs[:, 4 * qq + 2 * dh:4 * qq + 2 * dh + 2],
                                stat6[:, 0:6]))
                cmb = ln_combine(qq, 1)
                row = xrow0 + 512 * j + 128 * qq
                slot = qq % 2
                ld = sch.add("sync", dma(xres[slot][:], x_d[row:row + 128, :]),
                             waits=[xres_use[slot], xres_stat[slot]], sig=True, stream="ld")
                sch.add("vector",
                        lambda h, qq=qq, slot=slot: h.tensor_scalar(
                            xres[slot][:], xres[slot][:], lnc(qq, "nmr"), None, op0=ALU.add),
                        waits=[ld, cmb])
                fin = sch.add("vector",
                              lambda h, qq=qq, slot=slot: h.scalar_tensor_tensor(
                                  xres[slot][:], otmp[:], lnc(qq, "r"),
                                  xres[slot][:], op0=ALU.mult, op1=ALU.add),
                              sig=True)
                o_use[0] = fin
                xres_use[slot] = sch.add("sync", dma(x1_d[row:row + 128, :], xres[slot][:]),
                                         waits=[fin], sig=True, stream="st")
                x1_store_tok[4 * j + qq] = xres_use[slot]
                xres_stat[slot] = ln_stats(xres[slot][:], 4 + 4 * j + qq, [fin])
            sch.phase = 'p2'
        x1_done = [xres_use[0], xres_use[1]]
        attn_te = sch.token("tensor")
        attn_end = sch.barrier_tokens()
        sch.flush()
        ppS.close()
        p2.close()
        st_attn.close()

        p3 = ExitStack()
        otok = p3.enter_context(nc.sbuf_tensor(_un("otok"), [128, NT * D], F32, side="right"))
        ofs = [p3.enter_context(nc.sbuf_tensor(_un(f"ofs{k}"), [128, NCH * 512], BF16)) for k in range(4)]
        g1s = p3.enter_context(nc.sbuf_tensor(_un("g1s"), [128, NCH * 512], BF16))
        g2s = p3.enter_context(nc.sbuf_tensor(_un("g2s"), [128, 4 * 512], BF16))
        g3s = p3.enter_context(nc.sbuf_tensor(_un("g3s"), [128, 2 * 512], BF16))
        x1t = [p3.enter_context(nc.sbuf_tensor(_un(f"x1t{k}"), [128, D], F32)) for k in range(2)]
        wf1 = p3.enter_context(nc.sbuf_tensor(_un("wf1"), [128, NCH * D], BF16))
        wf2 = p3.enter_context(nc.sbuf_tensor(_un("wf2"), [128, NCH * 512], BF16))
        wf3 = p3.enter_context(nc.sbuf_tensor(_un("wf3"), [128, 4 * 256], BF16))
        wf4 = p3.enter_context(nc.sbuf_tensor(_un("wf4"), [128, 2], BF16))
        wg1 = p3.enter_context(nc.sbuf_tensor(_un("wg1"), [128, NCH * 512], BF16))
        wg2 = p3.enter_context(nc.sbuf_tensor(_un("wg2"), [128, 4 * 256], BF16))
        wg3 = p3.enter_context(nc.sbuf_tensor(_un("wg3"), [128, 2], BF16))
        pp3 = ExitStack()
        pG = [pp3.enter_context(nc.psum_tensor(_un(f"pG{k}"), [128, 512], F32)) for k in range(3)]
        pTf = pp3.enter_context(nc.psum_tensor(_un("pTf"), [128, 1024], F32))
        pRow = pp3.enter_context(nc.psum_tensor(_un("pRow"), [1, 512], F32))
        pSm = pp3.enter_context(nc.psum_tensor(_un("pSm"), [128, 8], F32))
        pCb = pp3.enter_context(nc.psum_tensor(_un("pCb"), [128, 8], BF16))
        ov = otok[:].rearrange("p (i n) -> p i n", n=D)
        ofv = [o[:].rearrange("p (c n) -> p c n", n=512) for o in ofs]
        g1v = g1s[:].rearrange("p (c n) -> p c n", n=512)
        g2v = g2s[:].rearrange("p (c n) -> p c n", n=512)
        g3v = g3s[:].rearrange("p (c n) -> p c n", n=512)
        wf1v = wf1[:].rearrange("p (d n) -> p d n", n=D)
        wf2v = wf2[:].rearrange("p (d n) -> p d n", n=512)
        wf3v = wf3[:].rearrange("p (d n) -> p d n", n=256)
        wf4v = wf4[:].rearrange("p (d n) -> p d n", n=1)
        wg1v = wg1[:].rearrange("p (d n) -> p d n", n=512)
        wg2v = wg2[:].rearrange("p (d n) -> p d n", n=256)
        wg3v = wg3[:].rearrange("p (d n) -> p d n", n=1)

        wf_tok = []
        for dst, src in ((wf1v, w1_d.rearrange("(d p) n -> p d n", p=128)),
                         (wf2v, w2_d.rearrange("(d p) n -> p d n", p=128)),
                         (wf3v, w3_d.rearrange("(d p) n -> p d n", p=128)),
                         (wf4v, w4_d.rearrange("(d p) n -> p d n", p=128)),
                         (wg1v, g1_d.rearrange("(d p) n -> p d n", p=128)),
                         (wg2v, g2_d.rearrange("(d p) n -> p d n", p=128)),
                         (wg3v, g3_d.rearrange("(d p) n -> p d n", p=128))):
            fn = dma_nc if dst in (wf4v, wg3v) else dma
            wf_tok.append(sch.add("gpsimd", fn(dst, src), waits=[attn_te], sig=True, stream="w"))

        comb3 = ln_combine(4, NT)
        aff3 = [None] * NT
        for i in range(NT):
            ld = sch.add("sync", dma(x1t[i % 2][:], x1_d[xrow0 + 128 * i:xrow0 + 128 * (i + 1), :]),
                         waits=[x1_store_tok[i], attn_te,
                                aff3[i - 2] if i >= 2 else None],
                         sig=True, stream="ld")
            aff3[i] = sch.add("scalar",
                              lambda h, i=i: h.activation(ov[:, i, :], x1t[i % 2][:],
                                                          AF.Identity,
                                                          bias=lnc(4 + i, "nmr"),
                                                          scale=lnc(4 + i, "r")),
                              waits=[ld, comb3] + (x1_done if i < 2 else []), sig=True)
        out_ready = [aff3[NT - 1]]
        ln2_aff_done = aff3[NT - 1]
        if debug and b == 0:
            o = dbg_out("out0", [128, NT * D], F32)
            sch.gate_ttl = 400; sch.extra_gate.append(sch.add("sync", dma(o, otok[:]), waits=[aff3[NT - 1]], sig=True, stream="st"))

        gf_sig = None
        pTf_use = [None]
        blk_ready = list(aff3)
        for cyc in range(3):
            gc_parts = []
            tf_sig = []
            slab_ready = [None] * 4
            for sl in range(4):
                for ib in range(4):
                    i = 4 * sl + ib
                    for c in range(8):
                        tr = sch.add("tensor",
                                lambda h, i=i, c=c: h.transpose(
                                    pTf[:, 128 * c:128 * (c + 1)],
                                    ov[:, i, 128 * c:128 * (c + 1)], c_idf[:]),
                                waits=[blk_ready[i], pTf_use[0]] if c == 0 else [],
                                sig=(c == 7))
                    pTf_use[0] = sch.add(
                        "vector",
                        lambda h, ib=ib, sl=sl: h.tensor_copy(
                            ofv[sl][:, :, 128 * ib:128 * (ib + 1)],
                            pTf[:].rearrange("p (g c) -> p g c", c=128)),
                        waits=[tr], sig=True)
                slab_ready[sl] = pTf_use[0]
            for sl in range(4):
                last_se = sch.token("scalar")
                slab_all = [slab_ready[sl]]
                g1_sig = []
                for jj in range(NCH):
                    ps = pG[jj % 3]
                    for d in range(NCH):
                        mm = sch.add("tensor",
                                lambda h, jj=jj, d=d, sl=sl, ps=ps: h.matmul(
                                    ps[:], wf1v[:, d, 128 * jj:128 * (jj + 1)],
                                    ofv[sl][:, d, :],
                                    start=(d == 0), stop=(d == NCH - 1)),
                                waits=([wf_tok[0], last_se] + slab_all +
                                       [g1_sig[jj - 3] if jj >= 3 else None]) if d == 0 else [],
                                sig=(d == NCH - 1))
                    g1_sig.append(sch.add("scalar",
                                          lambda h, jj=jj, ps=ps: h.activation(
                                              g1v[:, jj, :], ps[:], AF.Gelu,
                                              bias=b1_sb[:, jj:jj + 1]),
                                          waits=[mm], sig=True))
                g2_sig = []
                for jj in range(4):
                    ps = pG[jj % 3]
                    for d in range(NCH):
                        mm = sch.add("tensor",
                                lambda h, jj=jj, d=d, ps=ps: h.matmul(
                                    ps[:], wf2v[:, d, 128 * jj:128 * (jj + 1)],
                                    g1v[:, d, :],
                                    start=(d == 0), stop=(d == NCH - 1)),
                                waits=([wf_tok[1], g1_sig[-1],
                                        g2_sig[jj - 3] if jj >= 3 else None]) if d == 0 else [],
                                sig=(d == NCH - 1))
                    g2_sig.append(sch.add("scalar",
                                          lambda h, jj=jj, ps=ps: h.activation(
                                              g2v[:, jj, :], ps[:], AF.Gelu,
                                              bias=b2_sb[:, jj:jj + 1]),
                                          waits=[mm], sig=True))
                g3_sig = []
                for jj in range(2):
                    ps = pG[jj % 3]
                    for d in range(4):
                        mm = sch.add("tensor",
                                lambda h, jj=jj, d=d, ps=ps: h.matmul(
                                    ps[:], wf3v[:, d, 128 * jj:128 * (jj + 1)],
                                    g2v[:, d, :],
                                    start=(d == 0), stop=(d == 3)),
                                waits=([wf_tok[2], g2_sig[-1]]) if d == 0 else [],
                                sig=(d == 3))
                    g3_sig.append(sch.add("scalar",
                                          lambda h, jj=jj, ps=ps: h.activation(
                                              g3v[:, jj, :], ps[:], AF.Gelu,
                                              bias=b3_sb[:, jj:jj + 1]),
                                          waits=[mm], sig=True))
                for d in range(2):
                    mm = sch.add("tensor",
                            lambda h, d=d: h.matmul(pRow[:], wf4v[:, d, :], g3v[:, d, :],
                                                    start=(d == 0), stop=(d == 1)),
                            waits=([wf_tok[3], g3_sig[-1]] + tf_sig[-1:]) if d == 0 else [],
                            sig=(d == 1))
                th = sch.add("scalar",
                        lambda h, sl=sl: h.activation(
                            tfbuf[:, 512 * sl:512 * (sl + 1)], pRow[:],
                            AF.Tanh, bias=sc_sb2[:, 0:1], scale=0.5),
                        waits=[mm], sig=True)
                tf_sig.append(sch.add("vector",
                                      lambda h, sl=sl: h.tensor_scalar(
                                          tfbuf[:, 512 * sl:512 * (sl + 1)],
                                          tfbuf[:, 512 * sl:512 * (sl + 1)],
                                          0.5, 0.5, op0=ALU.mult, op1=ALU.add),
                                      waits=[th], sig=True))
                if cyc == 0:
                    gc_parts.append(sch.add(
                        "vector",
                        lambda h, sl=sl: h.reduce_sum(
                            gcp[:, 8 * sl:8 * (sl + 1)].rearrange("p (g c) -> p g c", c=1),
                            ofv[sl], axis=AX.X),
                        waits=slab_all, sig=True))
            if cyc == 0:
                sch.add("vector", lambda h: h.tensor_add(gcp[:, 0:8], gcp[:, 0:8],
                                                         gcp[:, 8:16]), waits=gc_parts)
                sch.add("vector", lambda h: h.tensor_add(gcp[:, 16:24], gcp[:, 16:24],
                                                         gcp[:, 24:32]))
                sch.add("vector", lambda h: h.tensor_add(gcp[:, 0:8], gcp[:, 0:8],
                                                         gcp[:, 16:24]))
                sch.add("vector", lambda h: h.tensor_scalar(gcp[:, 0:8], gcp[:, 0:8],
                                                            1.0 / S, None, op0=ALU.mult))
                gcast = sch.add("vector", lambda h: h.tensor_copy(gcc[:], gcp[:, 0:8]),
                                sig=True)
                for jj in range(4):
                    for d in range(NCH):
                        mm = sch.add("tensor",
                                lambda h, jj=jj, d=d: h.matmul(
                                    pSm[:, jj:jj + 1], wg1v[:, d, 128 * jj:128 * (jj + 1)],
                                    gcc[:, d:d + 1], start=(d == 0), stop=(d == NCH - 1)),
                                waits=[gcast, wf_tok[4]] if d == 0 else [],
                                sig=(jj == 3 and d == NCH - 1))
                gg1 = []
                for jj in range(4):
                    gg1.append(sch.add("scalar",
                                       lambda h, jj=jj: h.activation(
                                           g1gb[:, jj:jj + 1], pSm[:, jj:jj + 1], AF.Gelu,
                                           bias=gb1_sb[:, jj:jj + 1]),
                                       waits=[mm], sig=True))
                for jj in range(2):
                    for d in range(4):
                        mm = sch.add("tensor",
                                lambda h, jj=jj, d=d: h.matmul(
                                    pSm[:, 4 + jj:5 + jj], wg2v[:, d, 128 * jj:128 * (jj + 1)],
                                    g1gb[:, d:d + 1], start=(d == 0), stop=(d == 3)),
                                waits=[gg1[-1], wf_tok[5]] if d == 0 else [],
                                sig=(jj == 1 and d == 3))
                gg2 = []
                for jj in range(2):
                    gg2.append(sch.add("scalar",
                                       lambda h, jj=jj: h.activation(
                                           g2gb[:, jj:jj + 1], pSm[:, 4 + jj:5 + jj], AF.Gelu,
                                           bias=gb2_sb[:, jj:jj + 1]),
                                       waits=[mm], sig=True))
                for d in range(2):
                    mm = sch.add("tensor",
                            lambda h, d=d: h.matmul(pSm[0:1, 6:7], wg3v[:, d, :],
                                                    g2gb[:, d:d + 1],
                                                    start=(d == 0), stop=(d == 1)),
                            waits=[gg2[-1], wf_tok[6]] if d == 0 else [],
                            sig=(d == 1))
                th = sch.add("scalar",
                        lambda h: h.activation(scal[:, 0:1], pSm[0:1, 6:7],
                                               AF.Tanh, bias=sc_sb2[:, 1:2], scale=0.5),
                        waits=[mm], sig=True)
                gf_tok = sch.add("vector",
                                 lambda h: h.tensor_scalar(scal[:, 0:1], scal[:, 0:1],
                                                           0.5, 0.5, op0=ALU.mult,
                                                           op1=ALU.add),
                                 waits=[th], sig=True)
                sch.add("vector", lambda h: h.tensor_scalar(scal[:, 1:2], sc_sb[:, 2:3],
                                                            0.6, None, op0=ALU.mult),
                        waits=[gf_tok, tok_c])
                sch.add("vector", lambda h: h.tensor_scalar(scal[:, 3:4], scal[:, 0:1],
                                                            1.4, -1.0, op0=ALU.mult,
                                                            op1=ALU.add))
                sch.add("vector", lambda h: h.tensor_mul(scal[:, 3:4], scal[:, 3:4],
                                                         sc_sb[:, 2:3]))
                gf_sig = sch.add("vector", lambda h: h.tensor_scalar(scal[:, 2:3],
                                                                     scal[:, 3:4], 0.1,
                                                                     None, op0=ALU.add),
                                 sig=True)
            te_done = sch.token("tensor")
            up_sig = []
            up_blk = [None] * NT
            for sl in range(4):
                cr = sch.add("vector",
                             lambda h, sl=sl: h.tensor_scalar(
                                 crow[:], tfbuf[:, 512 * sl:512 * (sl + 1)],
                                 scal[:, 1:2], scal[:, 2:3], op0=ALU.mult, op1=ALU.add),
                             waits=[gf_sig, tf_sig[sl]], sig=True)
                cb = sch.add("vector", lambda h: h.tensor_copy(crowb[:], crow[:]),
                             waits=[cr], sig=True)
                trs = None
                for ib in range(4):
                    trs = sch.add("tensor",
                                  lambda h, ib=ib: h.transpose(
                                      pCb[:, 2 * ib:2 * ib + 1], crowb[:, 128 * ib:128 * (ib + 1)],
                                      c_idb[0:1, 0:1]),
                                  waits=[cb], sig=True)
                cc = sch.add("vector",
                             lambda h: h.tensor_copy(ccol[:].rearrange("p (g c) -> p g c", c=1),
                                                     pCb[:].rearrange("p (g c) -> p g c", c=2)[:, :, 0:1]),
                             waits=[trs], sig=True)
                nc_tok = sch.add("vector", lambda h: h.tensor_scalar(nccol[:], ccol[:],
                                                                     -1.0, None,
                                                                     op0=ALU.mult),
                                 waits=[cc], sig=True)
                for ib in range(4):
                    i = 4 * sl + ib
                    q = [ov[:, i, 0:256], ov[:, i, 256:512],
                         ov[:, i, 512:768], ov[:, i, 768:1024]]
                    cpos = ccol[:, ib:ib + 1]
                    cneg = nccol[:, ib:ib + 1]
                    w0 = [nc_tok, te_done, ln2_aff_done]
                    sch.add("vector", lambda h, i=i: h.tensor_copy(tmpq[:], ov[:, i, 0:512]),
                            waits=w0)
                    sch.add("vector", lambda h, q=q, c=cpos: h.scalar_tensor_tensor(
                        q[0], q[2], c, q[0], op0=ALU.mult, op1=ALU.add))
                    sch.add("vector", lambda h, q=q, c=cneg: h.scalar_tensor_tensor(
                        q[1], q[3], c, q[1], op0=ALU.mult, op1=ALU.add))
                    sch.add("vector", lambda h, q=q, c=cneg: h.scalar_tensor_tensor(
                        q[2], tmpq[:, 0:256], c, q[2], op0=ALU.mult, op1=ALU.add))
                    tok_u = sch.add("vector", lambda h, q=q, c=cpos: h.scalar_tensor_tensor(
                        q[3], tmpq[:, 256:512], c, q[3], op0=ALU.mult, op1=ALU.add), sig=True)
                    up_sig.append(tok_u)
                    up_blk[i] = tok_u
            out_ready = [up_sig[-1]]
            if debug and b == 0:
                o = dbg_out(f"outc{cyc}", [128, NT * D], F32)
                sch.gate_ttl = 400; sch.extra_gate.append(sch.add("sync", dma(o, otok[:]), waits=[up_sig[-1]], sig=True, stream="st"))
                o2 = dbg_out(f"tf{cyc}", [1, S], F32)
                sch.gate_ttl = 400; sch.extra_gate.append(sch.add("sync", dma(o2, tfbuf[:]), waits=[up_sig[-1]], sig=True, stream="st"))
                o3 = dbg_out(f"scal{cyc}", [1, 4], F32)
                sch.gate_ttl = 400; sch.extra_gate.append(sch.add("sync", dma(o3, scal[:, 0:4]), waits=[up_sig[-1]], sig=True, stream="st"))

        if b + 1 < NB:
            passA_next = emit_passA(b + 1, [aff_tok[NT - 1]])
        prev_compute = [sch.token("tensor"), sch.token("scalar"), sch.token("vector")]
        st4 = [None] * NT
        for i in range(NT):
            st4[i] = ln_stats(ov[:, i, :], i, out_ready)
        comb4 = ln_combine(0, NT)
        last_store = None
        fin_prev = [None, None]
        for i in range(NT):
            ld = sch.add("sync", dma(x1t[i % 2][:], x1_d[xrow0 + 128 * i:xrow0 + 128 * (i + 1), :]),
                         waits=[ln2_aff_done, st4[NT - 1],
                                fin_prev[i % 2]], sig=True, stream="ld")
            sch.add("vector",
                    lambda h, i=i: h.tensor_scalar(x1t[i % 2][:], x1t[i % 2][:],
                                                   lnc(i, "nmr"), None, op0=ALU.add),
                    waits=[ld, comb4])
            fin = sch.add("vector",
                          lambda h, i=i: h.scalar_tensor_tensor(
                              x1t[i % 2][:], ov[:, i, :], lnc(i, "r"),
                              x1t[i % 2][:], op0=ALU.mult, op1=ALU.add),
                          sig=True)
            last_store = sch.add("sync", dma(y_d[xrow0 + 128 * i:xrow0 + 128 * (i + 1), :],
                                             x1t[i % 2][:]),
                                 waits=[fin], sig=True, stream="st")
            fin_prev[i % 2] = last_store
        prev_batch_done = prev_compute
        prev_st_toks = [sch.token(ln) for ln in LANES
                        if ln.startswith("st") and sch.cnt[ln] > 0]
        prev_se_tok = sch.token("scalar")
        prev_ve_tok = sch.token("vector")
        sch.flush()
        pp3.close()
        p3.close()

    if debug:
        o = dbg_out("x1", [NB * S, D], F32)
        sch.gate_ttl = 400; sch.extra_gate.append(sch.add("sync", dma(o, x1_d), waits=sch.barrier_tokens(), sig=True, stream="st"))
    sch.flush()
    finals = [(ln, sch.cnt[ln]) for ln in LANES if ln.startswith("st") and sch.cnt[ln] > 0]
    with nc.Block(no_gpsimd_drain=True) as block:
        def fin_emit(h):
            for ln, v in finals:
                h.wait_ge(sch.sems[ln], v)
        block.sync(fin_emit)

    es.close()
    return nc


_CACHE = {}


def _get_nc():
    if "nc" not in _CACHE:
        _CACHE["nc"] = build_program()
    return _CACHE["nc"]


def kernel(**inputs):
    perm = head_perm()
    x = np.ascontiguousarray(np.asarray(inputs["L0_attn_x"], np.float32).reshape(16 * S, D))
    qw = np.ascontiguousarray(np.asarray(inputs["L0_attn_qw"], np.float32)[:, perm])
    kw = np.ascontiguousarray(np.asarray(inputs["L0_attn_kw"], np.float32)[:, perm])
    vw = np.ascontiguousarray(np.asarray(inputs["L0_attn_vw"], np.float32)[:, perm])
    ow = np.ascontiguousarray(np.asarray(inputs["L0_attn_ow"], np.float32)[perm, :])
    idb, idf, keep, ones_row, ind, ones128 = build_consts()
    sc = np.array([[float(np.asarray(inputs["L0_ffn_cn_b4"]).reshape(-1)[0]),
                    float(np.asarray(inputs["L0_ffn_gc_b3"]).reshape(-1)[0]),
                    float(np.asarray(inputs["L0_ffn_adaptive_range"]).reshape(-1)[0]),
                    0.0]], np.float32)
    common = {
        "qw": qw, "kw": kw, "vw": vw, "ow": ow,
        "w1": np.ascontiguousarray(np.asarray(inputs["L0_ffn_cn_w1"], np.float32)),
        "w2": np.ascontiguousarray(np.asarray(inputs["L0_ffn_cn_w2"], np.float32)),
        "w3": np.ascontiguousarray(np.asarray(inputs["L0_ffn_cn_w3"], np.float32)),
        "w4": np.ascontiguousarray(np.asarray(inputs["L0_ffn_cn_w4"], np.float32)),
        "g1": np.ascontiguousarray(np.asarray(inputs["L0_ffn_gc_w1"], np.float32)),
        "g2": np.ascontiguousarray(np.asarray(inputs["L0_ffn_gc_w2"], np.float32)),
        "g3": np.ascontiguousarray(np.asarray(inputs["L0_ffn_gc_w3"], np.float32)),
        "b1": np.ascontiguousarray(np.asarray(inputs["L0_ffn_cn_b1"], np.float32)),
        "b2": np.ascontiguousarray(np.asarray(inputs["L0_ffn_cn_b2"], np.float32)),
        "b3": np.ascontiguousarray(np.asarray(inputs["L0_ffn_cn_b3"], np.float32)),
        "gb1": np.ascontiguousarray(np.asarray(inputs["L0_ffn_gc_b1"], np.float32)),
        "gb2": np.ascontiguousarray(np.asarray(inputs["L0_ffn_gc_b2"], np.float32)),
        "sc": sc, "sc2": sc * np.array([[0.5, 0.5, 0, 0]], np.float32)[:, 0:4][:, 0:2] if False else np.ascontiguousarray(0.5 * sc[:, 0:2]),
        "cidb": idb, "cidf": idf, "ckeep": keep, "cones": ones_row, "cind": ind,
        "cone2": ones128,
    }
    in_maps = []
    for c in range(NCORES):
        m = dict(common)
        m["x"] = np.ascontiguousarray(x[c * NB * S:(c + 1) * NB * S, :])
        in_maps.append(m)
    nc = _get_nc()
    res = run_bass_kernel_spmd(nc, in_maps, core_ids=list(range(NCORES)))
    out = np.concatenate([np.asarray(res.results[c]["y"]) for c in range(NCORES)], axis=0)
    return out.reshape(16, S, D)


# version 19 (final)
# speedup vs baseline: 1.0306x; 1.0306x over previous
import numpy as np
import ml_dtypes
import concourse.bass as bass
import concourse.mybir as mybir
from concourse.bass_utils import run_bass_kernel_spmd
from contextlib import ExitStack

F32 = mybir.dt.float32
BF16 = mybir.dt.bfloat16
AF = mybir.ActivationFunctionType
ALU = mybir.AluOpType
AX = mybir.AxisListType
BF = ml_dtypes.bfloat16

NCORES = 8
NB = 2
S = 2048
D = 1024
H = 16
HD = 64
NCH = 8
NT = 16
NQT = 4
EPS = 1e-5
SERIALIZE = False

ENGINES = ("tensor", "vector", "scalar", "sync", "gpsimd")
DMA_ENG = ("sync", "gpsimd")
STREAM_LANES = {"ld": 6, "st": 6, "w": 3}
LANES = tuple(f"{s}{i}" for s, n in STREAM_LANES.items() for i in range(n))


class Sched:

    def __init__(self, nc):
        self.nc = nc
        self.ops = {e: [] for e in ENGINES}
        self.cnt = {e: 0 for e in ENGINES}
        for s in LANES:
            self.cnt[s] = 0
        self.rr = {s: 0 for s in STREAM_LANES}
        self.extra_gate = []
        self.gate_ttl = 0
        self.sems = {}

    SELF_SERIAL = ("vector", "scalar")
    TSER_PHASES = set()
    phase = ""

    def add(self, eng, fn, waits=(), sig=False, stream=None):
        assert (stream is not None) == (eng in DMA_ENG)
        if self.extra_gate:
            waits = list(waits) + self.extra_gate
            self.gate_ttl -= 1
            if self.gate_ttl <= 0:
                self.extra_gate = []
        ws = [w for w in waits if w is not None and w[0] != eng]
        lane = None
        if eng in DMA_ENG:
            lane = f"{stream}{self.rr[stream] % STREAM_LANES[stream]}"
            self.rr[stream] += 1
            if self.cnt[lane] > 0:
                ws.append((lane, self.cnt[lane]))
        if eng in self.SELF_SERIAL:
            if self.cnt[eng] > 0:
                ws.append((eng, self.cnt[eng]))
            sig = True
        if SERIALIZE:
            sig = True
            ws2 = [(e, self.cnt[e]) for e in ENGINES + LANES
                   if self.cnt[e] > 0 and e != eng and e != lane]
            if eng in self.SELF_SERIAL and self.cnt[eng] > 0:
                ws2.append((eng, self.cnt[eng]))
            if lane is not None and self.cnt[lane] > 0:
                ws2.append((lane, self.cnt[lane]))
            ws = ws2
        self.ops[eng].append((fn, ws, sig, lane))
        if sig:
            if eng in DMA_ENG:
                self.cnt[lane] += 16
                return (lane, self.cnt[lane])
            self.cnt[eng] += 1
            return (eng, self.cnt[eng])
        return None

    def token(self, eng):
        return (eng, self.cnt[eng]) if self.cnt[eng] > 0 else None

    def barrier_tokens(self):
        return [self.token(e) for e in ENGINES + LANES
                if e not in DMA_ENG and self.cnt[e] > 0]

    def flush(self):
        with self.nc.Block(no_gpsimd_drain=True) as block:
            for eng in ENGINES:
                ops = self.ops[eng]
                if not ops:
                    continue

                def make_emit(eng, ops):
                    def emit(h):
                        waited = {}
                        for fn, ws, sig in ops:
                            for (we, wv) in ws:
                                if waited.get(we, -1) < wv:
                                    h.wait_ge(self.sems[we], wv)
                                    waited[we] = wv
                            fn(h)
                            if sig:
                                pass
                    return emit

                def make_emit2(eng, ops):
                    def emit(h):
                        waited = {}
                        for fn, ws, sig, stream in ops:
                            for (we, wv) in ws:
                                if waited.get(we, -1) < wv:
                                    h.wait_ge(self.sems[we], wv)
                                    waited[we] = wv
                            ins = fn(h)
                            if sig:
                                if eng in DMA_ENG:
                                    ins.then_inc(self.sems[stream], 16)
                                else:
                                    ins.then_inc(self.sems[eng], 1)
                    return emit

                getattr(block, eng)(make_emit2(eng, ops))
                self.ops[eng] = []


def head_perm():
    p = np.empty(D, np.int64)
    for h in range(H):
        for j in range(HD):
            p[HD * h + j] = 16 * j + h
    return p


def build_consts():
    idb = np.eye(128, dtype=BF)
    idf = np.eye(128, dtype=np.float32)
    keep = np.triu(np.ones((128, 128), np.float32)).astype(BF)
    ones_row = np.ones((1, 128), BF)
    ones128 = np.ones((128, 128), BF)
    ind = np.zeros((128, NCH, 16), np.float32)
    for c in range(NCH):
        ind[0:64, c, 2 * c] = 1.0
        ind[64:128, c, 2 * c + 1] = 1.0
    ind = ind.reshape(128, 128).astype(BF)
    return idb, idf, keep, ones_row, ind, ones128


def build_program(debug=False):
    nc = bass.Bass("TRN2", target_bir_lowering=False)
    dbg = {}
    def dbg_out(name, shape, dtype):
        if debug and name not in dbg:
            dbg[name] = nc.dram_tensor("dbg_" + name, shape, dtype, kind="ExternalOutput").ap()
        return dbg.get(name)

    x_d = nc.dram_tensor("x", [NB * S, D], F32, kind="ExternalInput").ap()
    qw_d = nc.dram_tensor("qw", [D, D], F32, kind="ExternalInput").ap()
    kw_d = nc.dram_tensor("kw", [D, D], F32, kind="ExternalInput").ap()
    vw_d = nc.dram_tensor("vw", [D, D], F32, kind="ExternalInput").ap()
    ow_d = nc.dram_tensor("ow", [D, D], F32, kind="ExternalInput").ap()
    w1_d = nc.dram_tensor("w1", [D, D], F32, kind="ExternalInput").ap()
    w2_d = nc.dram_tensor("w2", [D, 512], F32, kind="ExternalInput").ap()
    w3_d = nc.dram_tensor("w3", [512, 256], F32, kind="ExternalInput").ap()
    w4_d = nc.dram_tensor("w4", [256, 1], F32, kind="ExternalInput").ap()
    g1_d = nc.dram_tensor("g1", [D, 512], F32, kind="ExternalInput").ap()
    g2_d = nc.dram_tensor("g2", [512, 256], F32, kind="ExternalInput").ap()
    g3_d = nc.dram_tensor("g3", [256, 1], F32, kind="ExternalInput").ap()
    b1_d = nc.dram_tensor("b1", [D], F32, kind="ExternalInput").ap()
    b2_d = nc.dram_tensor("b2", [512], F32, kind="ExternalInput").ap()
    b3_d = nc.dram_tensor("b3", [256], F32, kind="ExternalInput").ap()
    sc_d = nc.dram_tensor("sc", [1, 4], F32, kind="ExternalInput").ap()
    sc2_d = nc.dram_tensor("sc2", [1, 2], F32, kind="ExternalInput").ap()
    gb1_d = nc.dram_tensor("gb1", [512], F32, kind="ExternalInput").ap()
    gb2_d = nc.dram_tensor("gb2", [256], F32, kind="ExternalInput").ap()
    cidb_d = nc.dram_tensor("cidb", [128, 128], BF16, kind="ExternalInput").ap()
    cidf_d = nc.dram_tensor("cidf", [128, 128], F32, kind="ExternalInput").ap()
    ckeep_d = nc.dram_tensor("ckeep", [128, 128], BF16, kind="ExternalInput").ap()
    cones_d = nc.dram_tensor("cones", [1, 128], BF16, kind="ExternalInput").ap()
    cind_d = nc.dram_tensor("cind", [128, 128], BF16, kind="ExternalInput").ap()
    cone2_d = nc.dram_tensor("cone2", [128, 128], BF16, kind="ExternalInput").ap()
    y_d = nc.dram_tensor("y", [NB * S, D], F32, kind="ExternalOutput").ap()
    x1_d = nc.dram_tensor("x1buf", [NB * S, D], F32, kind="Internal").ap()

    es = ExitStack()
    es.enter_context(nc.allow_low_precision(
        "deliberate bf16 matmul-operand casts; fp32 accumulation kept in psum"))
    sch = Sched(nc)
    for e in ENGINES + LANES:
        if e in DMA_ENG:
            continue
        sch.sems[e] = es.enter_context(nc.semaphore("sem_" + e))
    _uid = [0]

    def _un(name):
        _uid[0] += 1
        return f"{name}_{_uid[0]}"

    c_idb = es.enter_context(nc.sbuf_tensor(_un("c_idb"), [128, 128], BF16))
    c_idf = es.enter_context(nc.sbuf_tensor(_un("c_idf"), [128, 128], F32))
    c_keep = es.enter_context(nc.sbuf_tensor(_un("c_keep"), [128, 128], BF16))
    c_ones = es.enter_context(nc.sbuf_tensor(_un("c_ones"), [1, 128], BF16))
    c_ind = es.enter_context(nc.sbuf_tensor(_un("c_ind"), [128, 128], BF16))
    c_one2 = es.enter_context(nc.sbuf_tensor(_un("c_one2"), [128, 128], BF16))
    b1_sb = es.enter_context(nc.sbuf_tensor(_un("b1_sb"), [128, 8], F32))
    b2_sb = es.enter_context(nc.sbuf_tensor(_un("b2_sb"), [128, 4], F32))
    b3_sb = es.enter_context(nc.sbuf_tensor(_un("b3_sb"), [128, 2], F32))
    gb1_sb = es.enter_context(nc.sbuf_tensor(_un("gb1_sb"), [128, 4], F32))
    gb2_sb = es.enter_context(nc.sbuf_tensor(_un("gb2_sb"), [128, 2], F32))
    sc_sb = es.enter_context(nc.sbuf_tensor(_un("sc_sb"), [1, 4], F32))
    sc_sb2 = es.enter_context(nc.sbuf_tensor(_un("sc_sb2"), [1, 2], F32))
    stat6 = es.enter_context(nc.sbuf_tensor(_un("stat6"), [128, 12], F32))
    aggs = es.enter_context(nc.sbuf_tensor(_un("aggs"), [128, 144], F32))
    lncol = es.enter_context(nc.sbuf_tensor(_un("lncol"), [128, 160], F32))
    negB = es.enter_context(nc.sbuf_tensor(_un("negB"), [128, 16], F32))
    srow = es.enter_context(nc.sbuf_tensor(_un("srow"), [16, 8], F32))
    Bb = es.enter_context(nc.sbuf_tensor(_un("Bb"), [16, 1], BF16))
    brow = es.enter_context(nc.sbuf_tensor(_un("brow"), [1, 16], BF16))
    tfbuf = es.enter_context(nc.sbuf_tensor(_un("tfbuf"), [1, S], F32))
    gcp = es.enter_context(nc.sbuf_tensor(_un("gcp"), [128, 32], F32))
    gcc = es.enter_context(nc.sbuf_tensor(_un("gcc"), [128, 8], BF16))
    g1gb = es.enter_context(nc.sbuf_tensor(_un("g1gb"), [128, 4], BF16))
    g2gb = es.enter_context(nc.sbuf_tensor(_un("g2gb"), [128, 2], BF16))
    scal = es.enter_context(nc.sbuf_tensor(_un("scal"), [1, 8], F32))
    crow = es.enter_context(nc.sbuf_tensor(_un("crow"), [1, 512], F32))
    crowb = es.enter_context(nc.sbuf_tensor(_un("crowb"), [1, 512], BF16))
    ccol = es.enter_context(nc.sbuf_tensor(_un("ccol"), [128, 4], F32))
    nccol = es.enter_context(nc.sbuf_tensor(_un("nccol"), [128, 4], F32))
    tmpq = es.enter_context(nc.sbuf_tensor(_un("tmpq"), [128, 512], F32))

    def dma(dst, src):
        return lambda h: h.dma_start(out=dst, in_=src)

    def dma_nc(dst, src):
        return lambda h: h.dma_start(out=dst, in_=src, allow_slow_non_contiguous=True)

    tok_c = None
    if debug:
        sch.add("vector", lambda h: h.memset(aggs[:], 0.0))
        sch.add("vector", lambda h: h.memset(lncol[:], 0.0))
    for dst, src in ((c_idb[:], cidb_d), (c_idf[:], cidf_d), (c_keep[:], ckeep_d),
                     (c_ones[:], cones_d), (c_ind[:], cind_d), (c_one2[:], cone2_d),
                     (sc_sb[:], sc_d), (sc_sb2[:], sc2_d)):
        tok_c = sch.add("sync", dma(dst, src), sig=True, stream="ld")
    for dst, src in ((b1_sb[:], b1_d.rearrange("(d p) -> p d", p=128)),
                     (b2_sb[:], b2_d.rearrange("(d p) -> p d", p=128)),
                     (b3_sb[:], b3_d.rearrange("(d p) -> p d", p=128)),
                     (gb1_sb[:], gb1_d.rearrange("(d p) -> p d", p=128)),
                     (gb2_sb[:], gb2_d.rearrange("(d p) -> p d", p=128))):
        tok_c = sch.add("sync", dma_nc(dst, src), sig=True, stream="ld")

    def ln_stats(src_ap, i, wait):
        sch.add("vector", lambda h, a=src_ap: h.bn_stats(stat6[:, 0:6], a[:, 0:512]),
                waits=wait)
        sch.add("vector", lambda h, a=src_ap: h.bn_stats(stat6[:, 6:12], a[:, 512:1024]))
        sch.add("vector", lambda h, i=i: h.bn_aggr(aggs[:, 4 * i:4 * i + 2], stat6[:, 0:6]))
        return sch.add("vector", lambda h, i=i: h.bn_aggr(aggs[:, 4 * i + 2:4 * i + 4],
                                                          stat6[:, 6:12]), sig=True)

    def ln_combine(lo, n):
        av = aggs[:].rearrange("p (i k) -> p i k", k=4)
        mA, vA = av[:, lo:lo + n, 0:1], av[:, lo:lo + n, 1:2]
        mB, vB = av[:, lo:lo + n, 2:3], av[:, lo:lo + n, 3:4]
        def flat(a):
            return a.rearrange("p i k -> p (i k)")
        m = lncol[:, 0 + lo:0 + lo + n]
        var = lncol[:, 40 + lo:40 + lo + n]
        nmr = lncol[:, 80 + lo:80 + lo + n]
        r = lncol[:, 120 + lo:120 + lo + n]
        sch.add("vector", lambda h: h.tensor_add(m, flat(mA), flat(mB)))
        sch.add("vector", lambda h: h.tensor_add(var, flat(vA), flat(vB)))
        sch.add("vector", lambda h: h.tensor_sub(nmr, flat(mA), flat(mB)))
        sch.add("vector", lambda h: h.tensor_mul(nmr, nmr, nmr))
        sch.add("vector", lambda h: h.tensor_scalar(m, m, 0.5, None, op0=ALU.mult))
        sch.add("vector", lambda h: h.tensor_scalar(var, var, 0.5, None, op0=ALU.mult))
        sch.add("vector", lambda h: h.scalar_tensor_tensor(var, nmr, 0.25, var,
                                                           op0=ALU.mult, op1=ALU.add))
        t0 = sch.add("vector", lambda h: h.tensor_scalar(var, var, EPS, None,
                                                         op0=ALU.add), sig=True)
        sch.add("scalar", lambda h: h.activation(r, var, AF.Ln), waits=[t0])
        t1 = sch.add("scalar", lambda h: h.activation(r, r, AF.Exp, scale=-0.5),
                     sig=True)
        return sch.add("vector", lambda h: h.scalar_tensor_tensor(nmr, m, -1.0, r,
                                                                  op0=ALU.mult, op1=ALU.mult),
                       waits=[t1], sig=True)

    def lnc(i, which):
        base = {"nmr": 80, "r": 120}[which]
        return lncol[:, base + i:base + i + 1]

    xt = [es.enter_context(nc.sbuf_tensor(_un(f"xt{k}"), [128, D], F32)) for k in range(2)]

    def emit_passA(b, gate):
        st_tok = [None] * NT
        for i in range(NT):
            w = [st_tok[i - 2]] if i >= 2 else list(gate)
            ld = sch.add("sync", dma(xt[i % 2][:],
                                     x_d[b * S + 128 * i:b * S + 128 * (i + 1), :]),
                         waits=w, sig=True, stream="ld")
            st_tok[i] = ln_stats(xt[i % 2][:], 20 + i, [ld])
        comb = ln_combine(20, NT)
        return st_tok, comb

    wbuf_use = [None, None]
    xres_use = [None, None]
    prev_batch_done = []
    prev_st_toks = []
    prev_se_tok = None
    prev_ve_tok = None

    for b in range(NB):
        xrow0 = b * S
        st_attn = ExitStack()
        Qf = st_attn.enter_context(nc.sbuf_tensor(_un("Qf"), [128, NCH * S], BF16))
        Kf = st_attn.enter_context(nc.sbuf_tensor(_un("Kf"), [128, NCH * S], BF16))
        Vp = st_attn.enter_context(nc.sbuf_tensor(_un("Vp"), [128, NT * H * 65], BF16))
        wo_sb = st_attn.enter_context(nc.sbuf_tensor(_un("wo_sb"), [128, NCH * D], BF16))
        Qv = Qf[:].rearrange("p (c t) -> p c t", t=S)
        Kv = Kf[:].rearrange("p (c t) -> p c t", t=S)
        Vv = Vp[:].rearrange("p (kb z) -> p kb z", z=H * 65)
        vz = Vp[:].rearrange("p (kb g z) -> p kb g z", g=H, z=65)
        Wov = wo_sb[:].rearrange("p (c n) -> p c n", n=D)

        p1 = ExitStack()
        hfeat = p1.enter_context(nc.sbuf_tensor(_un("hfeat"), [128, NCH * S], BF16))
        hv = hfeat[:].rearrange("p (c t) -> p c t", t=S)
        htok = [p1.enter_context(nc.sbuf_tensor(_un(f"htok{k}"), [128, D], BF16)) for k in range(2)]
        wbuf = [p1.enter_context(nc.sbuf_tensor(_un(f"wbuf{k}"), [128, 4096], BF16)) for k in range(2)]
        qsq = [p1.enter_context(nc.sbuf_tensor(_un(f"qsq{k}"), [128, S], BF16)) for k in range(2)]
        pp1 = ExitStack()
        pQ = [pp1.enter_context(nc.psum_tensor(_un(f"pQ{k}"), [128, 1024], F32)) for k in range(2)]
        pT = [pp1.enter_context(nc.psum_tensor(_un(f"pT{k}"), [128, 512], BF16)) for k in range(2)]

        sch.phase = 'p1tr'
        pT_use = [prev_ve_tok, prev_ve_tok]
        if b == 0:
            st_tok, comb1 = emit_passA(0, [tok_c])
        else:
            st_tok, comb1 = passA_next
        if debug and b == 0:
            o = dbg_out("aggs", [128, 144], F32)
            sch.gate_ttl = 400; sch.extra_gate.append(sch.add("sync", dma(o, aggs[:]), waits=[comb1], sig=True, stream="st"))
            o = dbg_out("lncol", [128, 160], F32)
            sch.gate_ttl = 400; sch.extra_gate.append(sch.add("sync", dma(o, lncol[:]), waits=[comb1], sig=True, stream="st"))
            o = dbg_out("xt0", [128, D], F32)
            sch.gate_ttl = 400; sch.extra_gate.append(sch.add("sync", dma(o, xt[1][:]), waits=[comb1], sig=True, stream="st"))
        aff_tok = [None] * NT
        tr_all = [None] * NT
        h_done = None
        for i in range(NT):
            w = [aff_tok[i - 2]] if i >= 2 else [st_tok[NT - 1]]
            ld = sch.add("sync", dma(xt[i % 2][:], x_d[xrow0 + 128 * i:xrow0 + 128 * (i + 1), :]),
                         waits=w, sig=True, stream="ld")
            aff_tok[i] = sch.add("scalar",
                                 lambda h, i=i: h.activation(htok[i % 2][:], xt[i % 2][:],
                                                             AF.Identity,
                                                             bias=lnc(20 + i, "nmr"),
                                                             scale=lnc(20 + i, "r")),
                                 waits=[ld, comb1, tr_all[i - 2] if i >= 2 else None],
                                 sig=True)
            for half in range(2):
                for c4 in range(4):
                    c = 4 * half + c4
                    tr = sch.add("tensor",
                            lambda h, i=i, c=c, half=half, c4=c4: h.transpose(
                                pT[half][:, 128 * c4:128 * (c4 + 1)],
                                htok[i % 2][:, 128 * c:128 * (c + 1)], c_idb[:]),
                            waits=[aff_tok[i], pT_use[half]] if c4 == 0 else [],
                            sig=(c4 == 3))
                pT_use[half] = sch.add("vector",
                                       lambda h, i=i, half=half: h.tensor_copy(
                                           hv[:, 4 * half:4 * half + 4, 128 * i:128 * (i + 1)],
                                           pT[half][:].rearrange("p (g c) -> p g c", c=128)),
                                       waits=[tr], sig=True)
            tr_all[i] = sch.token("tensor")
        h_done = sch.token("vector")

        def wsrc(wd, half):
            return wd.rearrange("(d p) n -> p d n", p=128)[:, :, 512 * half:512 * (half + 1)]

        sch.phase = 'proj'
        ones_set = sch.add("vector", lambda h: h.memset(vz[:, :, :, 64:65], 1.0),
                           waits=[tok_c], sig=True)
        units = [(wi, wd, half) for wi, wd in enumerate((qw_d, kw_d, vw_d))
                 for half in range(2)]
        load_tok = [None] * 6
        slot_mm_done = wbuf_use

        def issue_load(u):
            wi, wd, half = units[u]
            slot = u % 2
            load_tok[u] = sch.add(
                "gpsimd",
                dma(wbuf[slot][:].rearrange("p (d n) -> p d n", n=512), wsrc(wd, half)),
                waits=[slot_mm_done[slot]] + prev_batch_done, sig=True, stream="w")

        issue_load(0)
        issue_load(1)
        wo_ld = sch.add("gpsimd", dma(Wov, ow_d.rearrange("(c p) n -> p c n", p=128)),
                        waits=prev_batch_done, sig=True, stream="w")
        pq_use = [[prev_se_tok] + prev_st_toks, [prev_se_tok] + prev_st_toks]
        for u in range(6):
            wi, wd, half = units[u]
            slot = u % 2
            wb = wbuf[slot][:].rearrange("p (d n) -> p d n", n=512)
            wtk = load_tok[u]
            if wi < 2:
                dstv = Qv if wi == 0 else Kv
                for jp in range(2):
                    for tt in range(4):
                        pslot = (jp * 4 + tt) % 2
                        ps = pQ[pslot]
                        for jj in range(2):
                            jt = 2 * jp + jj
                            for d in range(NCH):
                                mm = sch.add("tensor",
                                        lambda h, wb=wb, d=d, jt=jt, jj=jj, tt=tt, ps=ps: h.matmul(
                                            ps[:, 512 * jj:512 * (jj + 1)],
                                            wb[:, d, 128 * jt:128 * (jt + 1)],
                                            hv[:, d, 512 * tt:512 * (tt + 1)],
                                            start=(d == 0), stop=(d == NCH - 1)),
                                        waits=([wtk, h_done] + (pq_use[pslot] if isinstance(pq_use[pslot], list) else [pq_use[pslot]])) if (d == 0 and jj == 0) else [],
                                        sig=(d == NCH - 1 and jj == 1))
                        dout = 4 * half + 2 * jp
                        pq_use[pslot] = sch.add(
                            "vector",
                            lambda h, dstv=dstv, dout=dout, tt=tt, ps=ps: h.tensor_copy(
                                dstv[:, dout:dout + 2, 512 * tt:512 * (tt + 1)],
                                ps[:].rearrange("p (g n) -> p g n", n=512)),
                            waits=[mm], sig=True)
            else:
                for kb in range(NT):
                    pslot = kb % 2
                    ps = pQ[pslot]
                    for d in range(NCH):
                        mm = sch.add("tensor",
                                lambda h, wb=wb, d=d, kb=kb, ps=ps: h.matmul(
                                    ps[:, 0:512], hv[:, d, 128 * kb:128 * (kb + 1)], wb[:, d, :],
                                    start=(d == 0), stop=(d == NCH - 1)),
                                waits=([wtk, h_done, ones_set] + (pq_use[pslot] if isinstance(pq_use[pslot], list) else [pq_use[pslot]])) if d == 0 else [],
                                sig=(d == NCH - 1))
                    pq_use[pslot] = sch.add(
                        "vector",
                        lambda h, kb=kb, half=half, ps=ps: h.tensor_copy(
                            vz[:, kb, 8 * half:8 * (half + 1), 0:64],
                            ps[:, 0:512].rearrange("p (g c) -> p g c", c=64)),
                        waits=[mm], sig=True)
            slot_mm_done[slot] = sch.token("tensor")
            if u + 2 < 6:
                issue_load(u + 2)
        v_done = sch.token("vector")
        wbuf_use = slot_mm_done

        sch.phase = 'p1b'
        pp1.close()
        pp1b = ExitStack()
        pN = [pp1b.enter_context(nc.psum_tensor(_un(f"pN{k}"), [16, 512], F32)) for k in range(4)]
        pBB = pp1b.enter_context(nc.psum_tensor(_un("pBB"), [128, 16], F32))
        pBr = pp1b.enter_context(nc.psum_tensor(_un("pBr"), [1, 16], BF16))
        indv = c_ind[:].rearrange("p (c z) -> p c z", z=16)
        prev_side = None
        prev_mm = [None, None]
        for src, col in ((Qv, 0), (Kv, 1)):
            sq_tok = [None, None]
            mm_tok = [None] * NCH
            for c in range(NCH):
                sq_tok[c % 2] = sch.add("scalar",
                                        lambda h, src=src, c=c: h.activation(
                                            qsq[c % 2][:], src[:, c, :], AF.Square),
                                        waits=[v_done,
                                               mm_tok[c - 2] if c >= 2 else prev_mm[c % 2]],
                                        sig=True)
                for tt in range(4):
                    mm_tok[c] = sch.add("tensor",
                            lambda h, c=c, tt=tt: h.matmul(
                                pN[tt][:], indv[:, c, :],
                                qsq[c % 2][:, 512 * tt:512 * (tt + 1)],
                                start=(c == 0), stop=(c == NCH - 1)),
                            waits=[sq_tok[c % 2], v_done, prev_side] if tt == 0 else [],
                            sig=(tt == 3))
            for tt in range(4):
                sch.add("vector",
                        lambda h, tt=tt: h.reduce_max(srow[0:16, 4 + tt:5 + tt],
                                                      pN[tt][:], axis=AX.X),
                        waits=[mm_tok[NCH - 1]])
            sch.add("vector", lambda h: h.tensor_max(srow[0:16, 4:5], srow[0:16, 4:5],
                                                     srow[0:16, 5:6]))
            sch.add("vector", lambda h: h.tensor_max(srow[0:16, 6:7], srow[0:16, 6:7],
                                                     srow[0:16, 7:8]))
            prev_side = sch.add("vector",
                                lambda h, col=col: h.tensor_max(srow[0:16, col:col + 1],
                                                                srow[0:16, 4:5],
                                                                srow[0:16, 6:7]), sig=True)
            prev_mm = [mm_tok[NCH - 2], mm_tok[NCH - 1]]
        t0 = sch.add("vector", lambda h: h.tensor_mul(srow[0:16, 2:3], srow[0:16, 0:1],
                                                      srow[0:16, 1:2]), sig=True)
        sch.add("scalar", lambda h: h.activation(srow[0:16, 3:4], srow[0:16, 2:3],
                                                 AF.Ln, scale=1.0 / 1024.0),
                waits=[t0])
        t1 = sch.add("scalar", lambda h: h.activation(srow[0:16, 3:4], srow[0:16, 3:4],
                                                      AF.Exp, scale=0.5),
                     waits=[], sig=True)
        t2 = sch.add("vector", lambda h: h.tensor_copy(Bb[:], srow[0:16, 3:4]),
                     waits=[t1], sig=True)
        t3 = sch.add("tensor", lambda h: h.transpose(pBr[:], Bb[:], c_idb[0:16, 0:16]),
                     waits=[t2], sig=True)
        t4 = sch.add("vector", lambda h: h.tensor_copy(brow[:], pBr[:]),
                     waits=[t3], sig=True)
        t5 = sch.add("tensor", lambda h: h.matmul(pBB[:], c_ones[:], brow[:],
                                                  start=True, stop=True),
                     waits=[t4], sig=True)
        negB_tok = sch.add("vector",
                           lambda h: h.tensor_scalar(negB[:], pBB[:], -1.0, -1.0,
                                                     op0=ALU.mult, op1=ALU.add),
                           waits=[t5], sig=True)
        if debug and b == 0:
            for nm, t, shp, dt in (("h", hfeat, [128, NCH * S], BF16),
                                   ("q", Qf, [128, NCH * S], BF16),
                                   ("k", Kf, [128, NCH * S], BF16),
                                   ("v", Vp, [128, NT * H * 65], BF16),
                                   ("negB", negB, [128, 16], F32)):
                o = dbg_out(nm, shp, dt)
                sch.gate_ttl = 400; sch.extra_gate.append(sch.add("sync", dma(o, t[:]), waits=[negB_tok, h_done],
                                              sig=True, stream="st"))
        sch.flush()
        pp1b.close()
        p1.close()

        sch.phase = 'p2'
        p2 = ExitStack()
        ringA = p2.enter_context(nc.sbuf_tensor(_un("ringA"), [128, 4 * 512], BF16))
        ringB = p2.enter_context(nc.sbuf_tensor(_un("ringB"), [128, 4 * 512], BF16))
        attnU = p2.enter_context(nc.sbuf_tensor(_un("attnU"), [128, 8 * 512], BF16))
        stage = p2.enter_context(nc.sbuf_tensor(_un("stage"), [64, 512], BF16))
        lbuf = p2.enter_context(nc.sbuf_tensor(_un("lbuf"), [65, 8 * 1024], BF16))
        rlb = p2.enter_context(nc.sbuf_tensor(_un("rlb"), [65, 8 * 1024], BF16))
        lscr = p2.enter_context(nc.sbuf_tensor(_un("lscr"), [65, 1024], F32))
        otmp = p2.enter_context(nc.sbuf_tensor(_un("otmp"), [128, 1024], F32))
        xres = [p2.enter_context(nc.sbuf_tensor(_un(f"xres{k}"), [128, D], F32)) for k in range(2)]
        attv = attnU[:].rearrange("p (g n) -> p g n", n=512)
        ringAv = ringA[:].rearrange("p (g n) -> p g n", n=512)
        ringBv = ringB[:].rearrange("p (g n) -> p g n", n=512)

        expA_done = [None, None]
        expB_done = [None, None]
        ringA_use = [None] * 4
        ringB_use = [None] * 4
        pv_free = None
        dmaB_prev = None
        o_use = [None, None]
        xres_stat = [None, None]
        rl_done = [None] * 8
        lb_use = [None] * 8
        lnl_prev = [None] * 8
        x1_store_tok = [None] * NT
        tt_carry = [None]
        ev_o = [None, None]
        rl_scr = None
        ppS = ExitStack()
        pSA = [ppS.enter_context(nc.psum_tensor(_un(f"pSA{k}"), [128, 512], F32)) for k in range(2)]
        pSB = [ppS.enter_context(nc.psum_tensor(_un(f"pSB{k}"), [128, 512], F32)) for k in range(2)]
        pPVA = ppS.enter_context(nc.psum_tensor(_un("pPVA"), [65, 512], F32))
        pPVB = ppS.enter_context(nc.psum_tensor(_un("pPVB"), [65, 512], F32))
        pO = [ppS.enter_context(nc.psum_tensor(_un(f"pO{k}"), [128, 512], F32)) for k in range(2)]
        for j in range(NQT):
            nch_j = 4 * j + 4
            ev_toks = []
            def s_mm(p, c, j=j):
                nonlocal s_tok
                ss = c % 2
                sch.add("tensor",
                        lambda h, p=p, c=c, ss=ss, j=j: h.matmul(
                            pSA[ss][:],
                            Kv[0:64, p, 128 * c:128 * (c + 1)],
                            Qv[0:64, p, 512 * j:512 * (j + 1)],
                            start=True, stop=True),
                        waits=[expA_done[ss], negB_tok])
                s_tok[c] = sch.add("tensor",
                        lambda h, p=p, c=c, ss=ss, j=j: h.matmul(
                            pSB[ss][:],
                            Kv[64:128, p, 128 * c:128 * (c + 1)],
                            Qv[64:128, p, 512 * j:512 * (j + 1)],
                            start=True, stop=True, tile_position=(64, 0)),
                        waits=[expB_done[ss]], sig=True)
            for p in range(8):
                hA, hB = 2 * p, 2 * p + 1
                pv_last = None
                s_tok = {}
                s_mm(p, 0)
                if nch_j > 1:
                    s_mm(p, 1)
                for c in range(nch_j):
                    ss = c % 2
                    r = c % 4
                    eA = sch.add("scalar",
                                 lambda h, r=r, ss=ss, hA=hA: h.activation(
                                     ringAv[:, r, :], pSA[ss][:], AF.Exp,
                                     bias=negB[:, hA:hA + 1], scale=1.0 / 32.0),
                                 waits=[s_tok[c], ringA_use[r]], sig=True)
                    expA_done[ss] = eA
                    eB = sch.add("scalar",
                                 lambda h, r=r, ss=ss, hB=hB: h.activation(
                                     ringBv[:, r, :], pSB[ss][:], AF.Exp,
                                     bias=negB[:, hB:hB + 1], scale=1.0 / 32.0),
                                 waits=[ringB_use[r]], sig=True)
                    expB_done[ss] = eB
                    if c + 2 < nch_j:
                        s_mm(p, c + 2)
                    rd = c - 4 * j
                    if 0 <= rd <= 3:
                        for ring in (ringAv, ringBv):
                            if rd > 0:
                                sch.add("vector",
                                        lambda h, ring=ring, r=r, rd=rd: h.memset(
                                            ring[:, r, 0:128 * rd], 0.0),
                                        waits=[eB])
                            sch.add("vector",
                                    lambda h, ring=ring, r=r, rd=rd: h.tensor_mul(
                                        ring[:, r, 128 * rd:128 * (rd + 1)],
                                        ring[:, r, 128 * rd:128 * (rd + 1)],
                                        c_keep[:]),
                                    waits=[eB])
                        mk = sch.add("vector", lambda h: h.nop(), sig=True)
                        eA = mk
                        eB = mk
                    sch.add("tensor",
                            lambda h, c=c, r=r, hA=hA, nch_j=nch_j: h.matmul(
                                pPVA[:], Vv[:, c, 65 * hA:65 * hA + 65],
                                ringAv[:, r, :],
                                start=(c == 0), stop=(c == nch_j - 1)),
                            waits=[eA, pv_free] if c == 0 else [eA])
                    pv_last = sch.add("tensor",
                            lambda h, c=c, r=r, hB=hB, nch_j=nch_j: h.matmul(
                                pPVB[:], Vv[:, c, 65 * hB:65 * hB + 65],
                                ringBv[:, r, :],
                                start=(c == 0), stop=(c == nch_j - 1)),
                            waits=[eB] if c == 0 else [eB], sig=True)
                    ringA_use[r] = pv_last
                    ringB_use[r] = pv_last
                sch.add("vector",
                        lambda h, p=p: h.tensor_copy(
                            lbuf[64:65, 1024 * p:1024 * p + 512], pPVA[64:65, :]),
                        waits=[pv_last, lb_use[p], lnl_prev[p]])
                sch.add("vector",
                        lambda h, p=p: h.tensor_copy(
                            lbuf[64:65, 1024 * p + 512:1024 * p + 1024], pPVB[64:65, :]))
                evA = sch.add("vector",
                              lambda h, p=p: h.tensor_copy(attv[0:64, p, :], pPVA[0:64, :]),
                              sig=True)
                evB = sch.add("vector",
                              lambda h: h.tensor_copy(stage[:], pPVB[0:64, :]),
                              waits=[dmaB_prev], sig=True)
                pv_free = evB
                dmaB_prev = sch.add("sync",
                                    lambda h, p=p: h.dma_start(out=attv[64:128, p, :],
                                                               in_=stage[:]),
                                    waits=[evB], sig=True, stream="st")
                ev_toks.append((evA, dmaB_prev))
                rl_done[p] = sch.token("vector")
            sch.phase = 'oproj'
            pBC = pO[0]
            lnl_tok = [None] * 8
            for p in range(8):
                sch.add("scalar",
                        lambda h, p=p: h.activation(
                            lscr[64:65, :],
                            lbuf[64:65, 1024 * p:1024 * p + 1024], AF.Ln),
                        waits=[rl_done[p], lnl_tok[p - 1] if p else None])
                lnl_tok[p] = sch.add("scalar",
                        lambda h, p=p: h.activation(
                            rlb[64:65, 1024 * p:1024 * p + 1024],
                            lscr[64:65, :], AF.Exp, scale=-1.0),
                        sig=True)
            tt_prev = tt_carry[0]
            for p in range(8):
                bmA = sch.add("tensor",
                              lambda h, p=p: h.matmul(
                                  pBC[:], c_one2[64:65, :],
                                  rlb[64:65, 1024 * p:1024 * p + 512],
                                  start=True, stop=True),
                              waits=[lnl_tok[p], tt_prev, ev_o[0]], sig=True)
                ttA = sch.add("vector",
                              lambda h, p=p: h.tensor_mul(attv[0:64, p, :],
                                                          attv[0:64, p, :], pBC[0:64, :]),
                              waits=[bmA, ev_toks[p][0]], sig=True)
                bmB = sch.add("tensor",
                              lambda h, p=p: h.matmul(
                                  pBC[:], c_one2[64:65, :],
                                  rlb[64:65, 1024 * p + 512:1024 * p + 1024],
                                  start=True, stop=True),
                              waits=[ttA], sig=True)
                tt_prev = sch.add("vector",
                                  lambda h, p=p: h.tensor_mul(attv[64:128, p, :],
                                                              attv[64:128, p, :],
                                                              pBC[64:128, :]),
                                  waits=[bmB, ev_toks[p][1]], sig=True)
                lb_use[p] = bmB
                lnl_prev[p] = lnl_tok[p]
            tt_carry[0] = tt_prev
            attn_ready = [tt_prev, dmaB_prev]
            for qq in range(4):
                for dh in range(2):
                    k = 2 * qq + dh
                    sl_o = k % 2
                    for pr in range(8):
                        mo_tok = sch.add("tensor",
                                lambda h, qq=qq, dh=dh, pr=pr, sl_o=sl_o: h.matmul(
                                    pO[sl_o][:],
                                    attv[:, pr, 128 * qq:128 * (qq + 1)],
                                    Wov[:, pr, 512 * dh:512 * (dh + 1)],
                                    start=(pr == 0), stop=(pr == 7)),
                                waits=(attn_ready + [wo_ld, ev_o[sl_o]]) if pr == 0 else [],
                                sig=(pr == 7))
                    ev_o[sl_o] = sch.add("vector",
                            lambda h, dh=dh, sl_o=sl_o: h.tensor_copy(
                                otmp[:, 512 * dh:512 * (dh + 1)], pO[sl_o][:]),
                            waits=[mo_tok], sig=True)
                for dh in range(2):
                    sch.add("vector",
                            lambda h, dh=dh: h.bn_stats(stat6[:, 0:6],
                                                        otmp[:, 512 * dh:512 * (dh + 1)]))
                    sch.add("vector",
                            lambda h, qq=qq, dh=dh: h.bn_aggr(
                                aggs[:, 4 * qq + 2 * dh:4 * qq + 2 * dh + 2],
                                stat6[:, 0:6]))
                cmb = ln_combine(qq, 1)
                row = xrow0 + 512 * j + 128 * qq
                slot = qq % 2
                ld = sch.add("sync", dma(xres[slot][:], x_d[row:row + 128, :]),
                             waits=[xres_use[slot], xres_stat[slot]], sig=True, stream="ld")
                sch.add("vector",
                        lambda h, qq=qq, slot=slot: h.tensor_scalar(
                            xres[slot][:], xres[slot][:], lnc(qq, "nmr"), None, op0=ALU.add),
                        waits=[ld, cmb])
                fin = sch.add("vector",
                              lambda h, qq=qq, slot=slot: h.scalar_tensor_tensor(
                                  xres[slot][:], otmp[:], lnc(qq, "r"),
                                  xres[slot][:], op0=ALU.mult, op1=ALU.add),
                              sig=True)
                o_use[0] = fin
                xres_use[slot] = sch.add("sync", dma(x1_d[row:row + 128, :], xres[slot][:]),
                                         waits=[fin], sig=True, stream="st")
                x1_store_tok[4 * j + qq] = xres_use[slot]
                xres_stat[slot] = ln_stats(xres[slot][:], 4 + 4 * j + qq, [fin])
            sch.phase = 'p2'
        x1_done = [xres_use[0], xres_use[1]]
        attn_te = sch.token("tensor")
        attn_end = sch.barrier_tokens()
        sch.flush()
        ppS.close()
        p2.close()
        st_attn.close()

        p3 = ExitStack()
        otok = p3.enter_context(nc.sbuf_tensor(_un("otok"), [128, NT * D], F32, side="right"))
        ofs = [p3.enter_context(nc.sbuf_tensor(_un(f"ofs{k}"), [128, NCH * 512], BF16)) for k in range(4)]
        g1s = p3.enter_context(nc.sbuf_tensor(_un("g1s"), [128, NCH * 512], BF16))
        g2s = p3.enter_context(nc.sbuf_tensor(_un("g2s"), [128, 4 * 512], BF16))
        g3s = p3.enter_context(nc.sbuf_tensor(_un("g3s"), [128, 2 * 512], BF16))
        x1t = [p3.enter_context(nc.sbuf_tensor(_un(f"x1t{k}"), [128, D], F32)) for k in range(2)]
        wf1 = p3.enter_context(nc.sbuf_tensor(_un("wf1"), [128, NCH * D], BF16))
        wf2 = p3.enter_context(nc.sbuf_tensor(_un("wf2"), [128, NCH * 512], BF16))
        wf3 = p3.enter_context(nc.sbuf_tensor(_un("wf3"), [128, 4 * 256], BF16))
        wf4 = p3.enter_context(nc.sbuf_tensor(_un("wf4"), [128, 2], BF16))
        wg1 = p3.enter_context(nc.sbuf_tensor(_un("wg1"), [128, NCH * 512], BF16))
        wg2 = p3.enter_context(nc.sbuf_tensor(_un("wg2"), [128, 4 * 256], BF16))
        wg3 = p3.enter_context(nc.sbuf_tensor(_un("wg3"), [128, 2], BF16))
        pp3 = ExitStack()
        pG = [pp3.enter_context(nc.psum_tensor(_un(f"pG{k}"), [128, 512], F32)) for k in range(3)]
        pTf = pp3.enter_context(nc.psum_tensor(_un("pTf"), [128, 1024], F32))
        pRow = pp3.enter_context(nc.psum_tensor(_un("pRow"), [1, 512], F32))
        pSm = pp3.enter_context(nc.psum_tensor(_un("pSm"), [128, 8], F32))
        pCb = pp3.enter_context(nc.psum_tensor(_un("pCb"), [128, 8], BF16))
        ov = otok[:].rearrange("p (i n) -> p i n", n=D)
        ofv = [o[:].rearrange("p (c n) -> p c n", n=512) for o in ofs]
        g1v = g1s[:].rearrange("p (c n) -> p c n", n=512)
        g2v = g2s[:].rearrange("p (c n) -> p c n", n=512)
        g3v = g3s[:].rearrange("p (c n) -> p c n", n=512)
        wf1v = wf1[:].rearrange("p (d n) -> p d n", n=D)
        wf2v = wf2[:].rearrange("p (d n) -> p d n", n=512)
        wf3v = wf3[:].rearrange("p (d n) -> p d n", n=256)
        wf4v = wf4[:].rearrange("p (d n) -> p d n", n=1)
        wg1v = wg1[:].rearrange("p (d n) -> p d n", n=512)
        wg2v = wg2[:].rearrange("p (d n) -> p d n", n=256)
        wg3v = wg3[:].rearrange("p (d n) -> p d n", n=1)

        wf_tok = []
        for dst, src in ((wf1v, w1_d.rearrange("(d p) n -> p d n", p=128)),
                         (wf2v, w2_d.rearrange("(d p) n -> p d n", p=128)),
                         (wf3v, w3_d.rearrange("(d p) n -> p d n", p=128)),
                         (wf4v, w4_d.rearrange("(d p) n -> p d n", p=128)),
                         (wg1v, g1_d.rearrange("(d p) n -> p d n", p=128)),
                         (wg2v, g2_d.rearrange("(d p) n -> p d n", p=128)),
                         (wg3v, g3_d.rearrange("(d p) n -> p d n", p=128))):
            fn = dma_nc if dst in (wf4v, wg3v) else dma
            wf_tok.append(sch.add("gpsimd", fn(dst, src), waits=[attn_te], sig=True, stream="w"))

        comb3 = ln_combine(4, NT)
        aff3 = [None] * NT
        for i in range(NT):
            ld = sch.add("sync", dma(x1t[i % 2][:], x1_d[xrow0 + 128 * i:xrow0 + 128 * (i + 1), :]),
                         waits=[x1_store_tok[i], attn_te,
                                aff3[i - 2] if i >= 2 else None],
                         sig=True, stream="ld")
            aff3[i] = sch.add("scalar",
                              lambda h, i=i: h.activation(ov[:, i, :], x1t[i % 2][:],
                                                          AF.Identity,
                                                          bias=lnc(4 + i, "nmr"),
                                                          scale=lnc(4 + i, "r")),
                              waits=[ld, comb3] + (x1_done if i < 2 else []), sig=True)
        out_ready = [aff3[NT - 1]]
        ln2_aff_done = aff3[NT - 1]
        if debug and b == 0:
            o = dbg_out("out0", [128, NT * D], F32)
            sch.gate_ttl = 400; sch.extra_gate.append(sch.add("sync", dma(o, otok[:]), waits=[aff3[NT - 1]], sig=True, stream="st"))

        gf_sig = None
        pTf_use = [None]
        blk_ready = list(aff3)
        for cyc in range(3):
            gc_parts = []
            tf_sig = []
            slab_ready = [None] * 4
            for sl in range(4):
                for ib in range(4):
                    i = 4 * sl + ib
                    for c in range(8):
                        tr = sch.add("tensor",
                                lambda h, i=i, c=c: h.transpose(
                                    pTf[:, 128 * c:128 * (c + 1)],
                                    ov[:, i, 128 * c:128 * (c + 1)], c_idf[:]),
                                waits=[blk_ready[i], pTf_use[0]] if c == 0 else [],
                                sig=(c == 7))
                    pTf_use[0] = sch.add(
                        "vector",
                        lambda h, ib=ib, sl=sl: h.tensor_copy(
                            ofv[sl][:, :, 128 * ib:128 * (ib + 1)],
                            pTf[:].rearrange("p (g c) -> p g c", c=128)),
                        waits=[tr], sig=True)
                slab_ready[sl] = pTf_use[0]
            for sl in range(4):
                last_se = sch.token("scalar")
                slab_all = [slab_ready[sl]]
                g1_sig = []
                for jj in range(NCH):
                    ps = pG[jj % 3]
                    for d in range(NCH):
                        mm = sch.add("tensor",
                                lambda h, jj=jj, d=d, sl=sl, ps=ps: h.matmul(
                                    ps[:], wf1v[:, d, 128 * jj:128 * (jj + 1)],
                                    ofv[sl][:, d, :],
                                    start=(d == 0), stop=(d == NCH - 1)),
                                waits=([wf_tok[0], last_se] + slab_all +
                                       [g1_sig[jj - 3] if jj >= 3 else None]) if d == 0 else [],
                                sig=(d == NCH - 1))
                    g1_sig.append(sch.add("scalar",
                                          lambda h, jj=jj, ps=ps: h.activation(
                                              g1v[:, jj, :], ps[:], AF.Gelu,
                                              bias=b1_sb[:, jj:jj + 1]),
                                          waits=[mm], sig=True))
                g2_sig = []
                for jj in range(4):
                    ps = pG[jj % 3]
                    for d in range(NCH):
                        mm = sch.add("tensor",
                                lambda h, jj=jj, d=d, ps=ps: h.matmul(
                                    ps[:], wf2v[:, d, 128 * jj:128 * (jj + 1)],
                                    g1v[:, d, :],
                                    start=(d == 0), stop=(d == NCH - 1)),
                                waits=([wf_tok[1], g1_sig[-1],
                                        g2_sig[jj - 3] if jj >= 3 else None]) if d == 0 else [],
                                sig=(d == NCH - 1))
                    g2_sig.append(sch.add("scalar",
                                          lambda h, jj=jj, ps=ps: h.activation(
                                              g2v[:, jj, :], ps[:], AF.Gelu,
                                              bias=b2_sb[:, jj:jj + 1]),
                                          waits=[mm], sig=True))
                g3_sig = []
                for jj in range(2):
                    ps = pG[jj % 3]
                    for d in range(4):
                        mm = sch.add("tensor",
                                lambda h, jj=jj, d=d, ps=ps: h.matmul(
                                    ps[:], wf3v[:, d, 128 * jj:128 * (jj + 1)],
                                    g2v[:, d, :],
                                    start=(d == 0), stop=(d == 3)),
                                waits=([wf_tok[2], g2_sig[-1]]) if d == 0 else [],
                                sig=(d == 3))
                    g3_sig.append(sch.add("scalar",
                                          lambda h, jj=jj, ps=ps: h.activation(
                                              g3v[:, jj, :], ps[:], AF.Gelu,
                                              bias=b3_sb[:, jj:jj + 1]),
                                          waits=[mm], sig=True))
                for d in range(2):
                    mm = sch.add("tensor",
                            lambda h, d=d: h.matmul(pRow[:], wf4v[:, d, :], g3v[:, d, :],
                                                    start=(d == 0), stop=(d == 1)),
                            waits=([wf_tok[3], g3_sig[-1]] + tf_sig[-1:]) if d == 0 else [],
                            sig=(d == 1))
                th = sch.add("scalar",
                        lambda h, sl=sl: h.activation(
                            tfbuf[:, 512 * sl:512 * (sl + 1)], pRow[:],
                            AF.Tanh, bias=sc_sb2[:, 0:1], scale=0.5),
                        waits=[mm], sig=True)
                tf_sig.append(sch.add("vector",
                                      lambda h, sl=sl: h.tensor_scalar(
                                          tfbuf[:, 512 * sl:512 * (sl + 1)],
                                          tfbuf[:, 512 * sl:512 * (sl + 1)],
                                          0.5, 0.5, op0=ALU.mult, op1=ALU.add),
                                      waits=[th], sig=True))
                if cyc == 0:
                    gc_parts.append(sch.add(
                        "vector",
                        lambda h, sl=sl: h.reduce_sum(
                            gcp[:, 8 * sl:8 * (sl + 1)].rearrange("p (g c) -> p g c", c=1),
                            ofv[sl], axis=AX.X),
                        waits=slab_all, sig=True))
            if cyc == 0:
                sch.add("vector", lambda h: h.tensor_add(gcp[:, 0:8], gcp[:, 0:8],
                                                         gcp[:, 8:16]), waits=gc_parts)
                sch.add("vector", lambda h: h.tensor_add(gcp[:, 16:24], gcp[:, 16:24],
                                                         gcp[:, 24:32]))
                sch.add("vector", lambda h: h.tensor_add(gcp[:, 0:8], gcp[:, 0:8],
                                                         gcp[:, 16:24]))
                sch.add("vector", lambda h: h.tensor_scalar(gcp[:, 0:8], gcp[:, 0:8],
                                                            1.0 / S, None, op0=ALU.mult))
                gcast = sch.add("vector", lambda h: h.tensor_copy(gcc[:], gcp[:, 0:8]),
                                sig=True)
                for jj in range(4):
                    for d in range(NCH):
                        mm = sch.add("tensor",
                                lambda h, jj=jj, d=d: h.matmul(
                                    pSm[:, jj:jj + 1], wg1v[:, d, 128 * jj:128 * (jj + 1)],
                                    gcc[:, d:d + 1], start=(d == 0), stop=(d == NCH - 1)),
                                waits=[gcast, wf_tok[4]] if d == 0 else [],
                                sig=(jj == 3 and d == NCH - 1))
                gg1 = []
                for jj in range(4):
                    gg1.append(sch.add("scalar",
                                       lambda h, jj=jj: h.activation(
                                           g1gb[:, jj:jj + 1], pSm[:, jj:jj + 1], AF.Gelu,
                                           bias=gb1_sb[:, jj:jj + 1]),
                                       waits=[mm], sig=True))
                for jj in range(2):
                    for d in range(4):
                        mm = sch.add("tensor",
                                lambda h, jj=jj, d=d: h.matmul(
                                    pSm[:, 4 + jj:5 + jj], wg2v[:, d, 128 * jj:128 * (jj + 1)],
                                    g1gb[:, d:d + 1], start=(d == 0), stop=(d == 3)),
                                waits=[gg1[-1], wf_tok[5]] if d == 0 else [],
                                sig=(jj == 1 and d == 3))
                gg2 = []
                for jj in range(2):
                    gg2.append(sch.add("scalar",
                                       lambda h, jj=jj: h.activation(
                                           g2gb[:, jj:jj + 1], pSm[:, 4 + jj:5 + jj], AF.Gelu,
                                           bias=gb2_sb[:, jj:jj + 1]),
                                       waits=[mm], sig=True))
                for d in range(2):
                    mm = sch.add("tensor",
                            lambda h, d=d: h.matmul(pSm[0:1, 6:7], wg3v[:, d, :],
                                                    g2gb[:, d:d + 1],
                                                    start=(d == 0), stop=(d == 1)),
                            waits=[gg2[-1], wf_tok[6]] if d == 0 else [],
                            sig=(d == 1))
                th = sch.add("scalar",
                        lambda h: h.activation(scal[:, 0:1], pSm[0:1, 6:7],
                                               AF.Tanh, bias=sc_sb2[:, 1:2], scale=0.5),
                        waits=[mm], sig=True)
                gf_tok = sch.add("vector",
                                 lambda h: h.tensor_scalar(scal[:, 0:1], scal[:, 0:1],
                                                           0.5, 0.5, op0=ALU.mult,
                                                           op1=ALU.add),
                                 waits=[th], sig=True)
                sch.add("vector", lambda h: h.tensor_scalar(scal[:, 1:2], sc_sb[:, 2:3],
                                                            0.6, None, op0=ALU.mult),
                        waits=[gf_tok, tok_c])
                sch.add("vector", lambda h: h.tensor_scalar(scal[:, 3:4], scal[:, 0:1],
                                                            1.4, -1.0, op0=ALU.mult,
                                                            op1=ALU.add))
                sch.add("vector", lambda h: h.tensor_mul(scal[:, 3:4], scal[:, 3:4],
                                                         sc_sb[:, 2:3]))
                gf_sig = sch.add("vector", lambda h: h.tensor_scalar(scal[:, 2:3],
                                                                     scal[:, 3:4], 0.1,
                                                                     None, op0=ALU.add),
                                 sig=True)
            te_done = sch.token("tensor")
            up_sig = []
            up_blk = [None] * NT
            for sl in range(4):
                cr = sch.add("vector",
                             lambda h, sl=sl: h.tensor_scalar(
                                 crow[:], tfbuf[:, 512 * sl:512 * (sl + 1)],
                                 scal[:, 1:2], scal[:, 2:3], op0=ALU.mult, op1=ALU.add),
                             waits=[gf_sig, tf_sig[sl]], sig=True)
                cb = sch.add("vector", lambda h: h.tensor_copy(crowb[:], crow[:]),
                             waits=[cr], sig=True)
                trs = None
                for ib in range(4):
                    trs = sch.add("tensor",
                                  lambda h, ib=ib: h.transpose(
                                      pCb[:, 2 * ib:2 * ib + 1], crowb[:, 128 * ib:128 * (ib + 1)],
                                      c_idb[0:1, 0:1]),
                                  waits=[cb], sig=True)
                cc = sch.add("vector",
                             lambda h: h.tensor_copy(ccol[:].rearrange("p (g c) -> p g c", c=1),
                                                     pCb[:].rearrange("p (g c) -> p g c", c=2)[:, :, 0:1]),
                             waits=[trs], sig=True)
                nc_tok = sch.add("vector", lambda h: h.tensor_scalar(nccol[:], ccol[:],
                                                                     -1.0, None,
                                                                     op0=ALU.mult),
                                 waits=[cc], sig=True)
                for ib in range(4):
                    i = 4 * sl + ib
                    q = [ov[:, i, 0:256], ov[:, i, 256:512],
                         ov[:, i, 512:768], ov[:, i, 768:1024]]
                    cpos = ccol[:, ib:ib + 1]
                    cneg = nccol[:, ib:ib + 1]
                    w0 = [nc_tok, te_done, ln2_aff_done]
                    sch.add("vector", lambda h, i=i: h.tensor_copy(tmpq[:], ov[:, i, 0:512]),
                            waits=w0)
                    sch.add("vector", lambda h, q=q, c=cpos: h.scalar_tensor_tensor(
                        q[0], q[2], c, q[0], op0=ALU.mult, op1=ALU.add))
                    sch.add("vector", lambda h, q=q, c=cneg: h.scalar_tensor_tensor(
                        q[1], q[3], c, q[1], op0=ALU.mult, op1=ALU.add))
                    sch.add("vector", lambda h, q=q, c=cneg: h.scalar_tensor_tensor(
                        q[2], tmpq[:, 0:256], c, q[2], op0=ALU.mult, op1=ALU.add))
                    tok_u = sch.add("vector", lambda h, q=q, c=cpos: h.scalar_tensor_tensor(
                        q[3], tmpq[:, 256:512], c, q[3], op0=ALU.mult, op1=ALU.add), sig=True)
                    up_sig.append(tok_u)
                    up_blk[i] = tok_u
            out_ready = [up_sig[-1]]
            if debug and b == 0:
                o = dbg_out(f"outc{cyc}", [128, NT * D], F32)
                sch.gate_ttl = 400; sch.extra_gate.append(sch.add("sync", dma(o, otok[:]), waits=[up_sig[-1]], sig=True, stream="st"))
                o2 = dbg_out(f"tf{cyc}", [1, S], F32)
                sch.gate_ttl = 400; sch.extra_gate.append(sch.add("sync", dma(o2, tfbuf[:]), waits=[up_sig[-1]], sig=True, stream="st"))
                o3 = dbg_out(f"scal{cyc}", [1, 4], F32)
                sch.gate_ttl = 400; sch.extra_gate.append(sch.add("sync", dma(o3, scal[:, 0:4]), waits=[up_sig[-1]], sig=True, stream="st"))

        if b + 1 < NB:
            passA_next = emit_passA(b + 1, [aff_tok[NT - 1]])
        prev_compute = [sch.token("tensor"), sch.token("scalar"), sch.token("vector")]
        st4 = [None] * NT
        for i in range(NT):
            st4[i] = ln_stats(ov[:, i, :], i, out_ready)
        comb4 = ln_combine(0, NT)
        last_store = None
        fin_prev = [None, None]
        for i in range(NT):
            ld = sch.add("sync", dma(x1t[i % 2][:], x1_d[xrow0 + 128 * i:xrow0 + 128 * (i + 1), :]),
                         waits=[ln2_aff_done, st4[NT - 1],
                                fin_prev[i % 2]], sig=True, stream="ld")
            sch.add("vector",
                    lambda h, i=i: h.tensor_scalar(x1t[i % 2][:], x1t[i % 2][:],
                                                   lnc(i, "nmr"), None, op0=ALU.add),
                    waits=[ld, comb4])
            fin = sch.add("vector",
                          lambda h, i=i: h.scalar_tensor_tensor(
                              x1t[i % 2][:], ov[:, i, :], lnc(i, "r"),
                              x1t[i % 2][:], op0=ALU.mult, op1=ALU.add),
                          sig=True)
            last_store = sch.add("sync", dma(y_d[xrow0 + 128 * i:xrow0 + 128 * (i + 1), :],
                                             x1t[i % 2][:]),
                                 waits=[fin], sig=True, stream="st")
            fin_prev[i % 2] = last_store
        prev_batch_done = prev_compute
        prev_st_toks = [sch.token(ln) for ln in LANES
                        if ln.startswith("st") and sch.cnt[ln] > 0]
        prev_se_tok = sch.token("scalar")
        prev_ve_tok = sch.token("vector")
        sch.flush()
        pp3.close()
        p3.close()

    if debug:
        o = dbg_out("x1", [NB * S, D], F32)
        sch.gate_ttl = 400; sch.extra_gate.append(sch.add("sync", dma(o, x1_d), waits=sch.barrier_tokens(), sig=True, stream="st"))
    sch.flush()
    finals = [(ln, sch.cnt[ln]) for ln in LANES if ln.startswith("st") and sch.cnt[ln] > 0]
    with nc.Block(no_gpsimd_drain=True) as block:
        def fin_emit(h):
            for ln, v in finals:
                h.wait_ge(sch.sems[ln], v)
        block.sync(fin_emit)

    es.close()
    return nc


_CACHE = {}


def _get_nc():
    if "nc" not in _CACHE:
        _CACHE["nc"] = build_program()
    return _CACHE["nc"]


def kernel(**inputs):
    perm = head_perm()
    x = np.ascontiguousarray(np.asarray(inputs["L0_attn_x"], np.float32).reshape(16 * S, D))
    qw = np.ascontiguousarray(np.asarray(inputs["L0_attn_qw"], np.float32)[:, perm])
    kw = np.ascontiguousarray(np.asarray(inputs["L0_attn_kw"], np.float32)[:, perm])
    vw = np.ascontiguousarray(np.asarray(inputs["L0_attn_vw"], np.float32)[:, perm])
    ow = np.ascontiguousarray(np.asarray(inputs["L0_attn_ow"], np.float32)[perm, :])
    idb, idf, keep, ones_row, ind, ones128 = build_consts()
    sc = np.array([[float(np.asarray(inputs["L0_ffn_cn_b4"]).reshape(-1)[0]),
                    float(np.asarray(inputs["L0_ffn_gc_b3"]).reshape(-1)[0]),
                    float(np.asarray(inputs["L0_ffn_adaptive_range"]).reshape(-1)[0]),
                    0.0]], np.float32)
    common = {
        "qw": qw, "kw": kw, "vw": vw, "ow": ow,
        "w1": np.ascontiguousarray(np.asarray(inputs["L0_ffn_cn_w1"], np.float32)),
        "w2": np.ascontiguousarray(np.asarray(inputs["L0_ffn_cn_w2"], np.float32)),
        "w3": np.ascontiguousarray(np.asarray(inputs["L0_ffn_cn_w3"], np.float32)),
        "w4": np.ascontiguousarray(np.asarray(inputs["L0_ffn_cn_w4"], np.float32)),
        "g1": np.ascontiguousarray(np.asarray(inputs["L0_ffn_gc_w1"], np.float32)),
        "g2": np.ascontiguousarray(np.asarray(inputs["L0_ffn_gc_w2"], np.float32)),
        "g3": np.ascontiguousarray(np.asarray(inputs["L0_ffn_gc_w3"], np.float32)),
        "b1": np.ascontiguousarray(np.asarray(inputs["L0_ffn_cn_b1"], np.float32)),
        "b2": np.ascontiguousarray(np.asarray(inputs["L0_ffn_cn_b2"], np.float32)),
        "b3": np.ascontiguousarray(np.asarray(inputs["L0_ffn_cn_b3"], np.float32)),
        "gb1": np.ascontiguousarray(np.asarray(inputs["L0_ffn_gc_b1"], np.float32)),
        "gb2": np.ascontiguousarray(np.asarray(inputs["L0_ffn_gc_b2"], np.float32)),
        "sc": sc, "sc2": sc * np.array([[0.5, 0.5, 0, 0]], np.float32)[:, 0:4][:, 0:2] if False else np.ascontiguousarray(0.5 * sc[:, 0:2]),
        "cidb": idb, "cidf": idf, "ckeep": keep, "cones": ones_row, "cind": ind,
        "cone2": ones128,
    }
    in_maps = []
    for c in range(NCORES):
        m = dict(common)
        m["x"] = np.ascontiguousarray(x[c * NB * S:(c + 1) * NB * S, :])
        in_maps.append(m)
    nc = _get_nc()
    res = run_bass_kernel_spmd(nc, in_maps, core_ids=list(range(NCORES)))
    out = np.concatenate([np.asarray(res.results[c]["y"]) for c in range(NCORES)], axis=0)
    return out.reshape(16, S, D)
